# version 07
# baseline (speedup 1.0000x reference)
import numpy as np
import ml_dtypes
import concourse.bass as bass
import concourse.mybir as mybir
from concourse.bass_utils import run_bass_kernel_spmd

F32 = mybir.dt.float32
BF16 = mybir.dt.bfloat16
AF = mybir.ActivationFunctionType
ALU = mybir.AluOpType
AX = mybir.AxisListType

B, S, D = 4, 8192, 1024
H = 16
CK = 128
N_PRE = 31
N_ATTN = 34
N_WIN = 17
DH = 4096
NCC = 32
MAGIC = float(np.float32(1.5 * 2.0 ** 23))
LN8INV = float(np.log(0.125))
KCOL_VALS = [1.0, -1.0, -0.5, 1.0 / 1024.0, 1.0 / 64.0, 1e-6, 1e-5, LN8INV, MAGIC,
             -1e-5, 4.8441870864585871, -1.0 / 1024.0]

bf16 = ml_dtypes.bfloat16


class Sem:
    __slots__ = ("h", "name", "n")

    def __init__(self, handle, name):
        self.h = handle
        self.name = name
        self.n = 0

    def inc(self, k=1):
        self.n += k
        return (self, self.n)

    @property
    def tok(self):
        return (self, self.n)


ZERO_TOK = None


def build_nc(dbg_y1=False, dbg_chunk=None, serialize=False, dbg_win=None, dbg_wq=False, dbg_scan=False):
    nc = bass.Bass("TRN2", target_bir_lowering=False)

    x_attn = nc.dram_tensor("x_attn", [N_ATTN * CK, D], F32, kind="ExternalInput").ap()
    x_pre = nc.dram_tensor("x_pre", [N_PRE * CK, D], F32, kind="ExternalInput").ap()
    wq_d = nc.dram_tensor("wq", [D, D], F32, kind="ExternalInput").ap()
    wk_d = nc.dram_tensor("wk", [D, D], F32, kind="ExternalInput").ap()
    wv_d = nc.dram_tensor("wv", [D, D], F32, kind="ExternalInput").ap()
    wg_d = nc.dram_tensor("wg", [D, D], F32, kind="ExternalInput").ap()
    wgt_d = nc.dram_tensor("wgt", [D, H], F32, kind="ExternalInput").ap()
    wo_d = nc.dram_tensor("wo", [D, D], F32, kind="ExternalInput").ap()
    w1t_d = nc.dram_tensor("w1t", [D, DH], F32, kind="ExternalInput").ap()
    w2t_d = nc.dram_tensor("w2t", [DH, D], F32, kind="ExternalInput").ap()
    cw_d = nc.dram_tensor("cw", [128, 3 * NCC], F32, kind="ExternalInput").ap()
    cb_d = nc.dram_tensor("cb", [128, NCC], F32, kind="ExternalInput").ap()
    ident_d = nc.dram_tensor("ident", [128, 128], BF16, kind="ExternalInput").ap()
    identf_d = nc.dram_tensor("identf", [128, 128], F32, kind="ExternalInput").ap()
    mask4_d = nc.dram_tensor("mask4", [128, 512], BF16, kind="ExternalInput").ap()
    tri_d = nc.dram_tensor("tri", [128, 128], F32, kind="ExternalInput").ap()
    onesf_d = nc.dram_tensor("onesf", [128, 128], F32, kind="ExternalInput").ap()
    onesr_d = nc.dram_tensor("onesr", [1, 128], F32, kind="ExternalInput").ap()
    onesc_d = nc.dram_tensor("onesc", [128, 1], F32, kind="ExternalInput").ap()
    kcol_d = nc.dram_tensor("kcol", [128, len(KCOL_VALS)], F32, kind="ExternalInput").ap()

    out_d = nc.dram_tensor("out", [4096, D], F32, kind="ExternalOutput").ap()
    y1buf = nc.dram_tensor("y1buf", [N_ATTN * CK, D], F32,
                           kind="ExternalOutput" if dbg_y1 else "Internal").ap()
    wq_kind = "ExternalOutput" if (dbg_win is not None or dbg_wq) else "Internal"
    w1q_h = nc.dram_tensor("w1q_h", [D, DH], BF16, kind=wq_kind).ap()
    w2q_h = nc.dram_tensor("w2q_h", [DH, D], BF16, kind=wq_kind).ap()
    dbg_d = {}
    if dbg_scan:
        dbg_d["Sdump"] = nc.dram_tensor("Sdump", [65 * 128, 512], F32, kind="ExternalOutput").ap()
        dbg_d["Cdump"] = nc.dram_tensor("Cdump", [65 * 128, 16], F32, kind="ExternalOutput").ap()
        dbg_d["Edump"] = nc.dram_tensor("Edump", [65 * 128, 32], F32, kind="ExternalOutput").ap()
        dbg_d["Ldump"] = nc.dram_tensor("Ldump", [65 * 128, 32], F32, kind="ExternalOutput").ap()
        dbg_d["Tdump"] = nc.dram_tensor("Tdump", [128, 256], F32, kind="ExternalOutput").ap()
        dbg_d["Bdump"] = nc.dram_tensor("Bdump", [65 * 128, 16], F32, kind="ExternalOutput").ap()
    if dbg_win is not None:
        for nm, wdt, wd in [("ytfm", BF16, 2048), ("ext", BF16, NCC * 258), ("conv", BF16, NCC * 256),
                            ("h2q", BF16, NCC * 256), ("bcols", F32, 16),
                            ("y2a", F32, 1024), ("s2b", F32, 256), ("silu", BF16, NCC * 256)]:
            dbg_d[nm] = nc.dram_tensor("dbg_" + nm, [128, wd], wdt,
                                       kind="ExternalOutput").ap()
    if dbg_chunk is not None:
        for nm, wdt, wd in [("q_tm", BF16, 1024), ("k_tm", BF16, 1024), ("v_tm", BF16, 1024),
                            ("g_f", F32, 1024), ("lnw", F32, 16), ("epos", F32, 16),
                            ("eneg", F32, 16), ("eB", F32, 16), ("eBmb", F32, 16),
                            ("kdec", BF16, 1024), ("kd_tm", BF16, 1024), ("qfm", BF16, 1024),
                            ("kfm", BF16, 1024), ("A_sb", BF16, 2048), ("o_tm", F32, 1024),
                            ("ro", F32, 16), ("sg", F32, 1024), ("gated", BF16, 1024),
                            ("y1t", F32, 1024), ("cols", F32, 16), ("sO", F32, 16),
                            ("S_old", BF16, 512), ("S_new", F32, 512), ("sso", F32, 16)]:
            dbg_d[nm] = nc.dram_tensor("dbg_" + nm, [128, wd], wdt,
                                       kind="ExternalOutput").ap()

    st_T, st_V, st_S, st_Y, st_G = [], [], [], [], []

    def T(f): st_T.append(f)
    def V(f): st_V.append(f)
    def SC(f): st_S.append(f)
    def SY(f): st_Y.append(f)
    def G(f): st_G.append(f)

    def wt(stream, tok):
        if tok is None:
            return
        sem, val = tok
        assert val <= sem.n, f"wait on future: {sem.name} {val} > {sem.n}"
        if val > 0:
            stream(lambda e, s=sem, v=val: e.wait_ge(s.h, v))

    from contextlib import ExitStack
    g_stack = ExitStack()

    def mk_sem(name):
        return Sem(g_stack.enter_context(nc.semaphore(name)), name)

    s_x = mk_sem("s_x")
    s_xb = mk_sem("s_xb")
    s_w = mk_sem("s_w")
    s_sl = mk_sem("s_sl")
    s_qw = mk_sem("s_qw")
    s_y1 = mk_sem("s_y1")
    s_y1r = mk_sem("s_y1r")
    s_out = mk_sem("s_out")
    s_T = mk_sem("s_T")
    s_V = mk_sem("s_V")
    s_S = mk_sem("s_S")
    s_bar = mk_sem("s_bar")
    s_dbg = mk_sem("s_dbg")
    s_ser = mk_sem("s_ser")
    ser_n = [0]

    def ser_barrier():
        if not serialize:
            return
        for st_, eng in ((T, "t"), (V, "v"), (SC, "s")):
            if eng == "t":
                wt(T, psm_free[0])
                st_(lambda e: e.matmul(PSM[:, 101:102], onesr[:], onesc[0:1, 0:1],
                                       start=True, stop=True).then_inc(s_ser.h, 1))
            elif eng == "v":
                st_(lambda e: e.engine_nop().then_inc(s_ser.h, 1))
            else:
                st_(lambda e: e.memzero(u16[0:1, 0:1]).then_inc(s_ser.h, 1))
        s_ser.inc(3)
        ser_n[0] += 3
        for st_ in (T, V, SC):
            wt(st_, (s_ser, ser_n[0]))

    PSA = g_stack.enter_context(nc.psum_tensor("PSA", [128, 1024], F32))
    PSB = g_stack.enter_context(nc.psum_tensor("PSB", [128, 1024], F32))
    PSC = g_stack.enter_context(nc.psum_tensor("PSC", [128, 512], F32))
    PSD = g_stack.enter_context(nc.psum_tensor("PSD", [128, 512], F32))
    PTR = g_stack.enter_context(nc.psum_tensor("PTR", [128, 1024], BF16))
    PSM = g_stack.enter_context(nc.psum_tensor("PSM", [128, 512], F32))

    ident = g_stack.enter_context(nc.sbuf_tensor("ident_s", [128, 128], BF16))
    identf = g_stack.enter_context(nc.sbuf_tensor("identf_s", [128, 128], F32))
    onesr = g_stack.enter_context(nc.sbuf_tensor("onesr_s", [1, 128], F32))
    onesc = g_stack.enter_context(nc.sbuf_tensor("onesc_s", [128, 1], F32))
    cw_sb = g_stack.enter_context(nc.sbuf_tensor("cw_sb", [128, 3 * NCC], F32))
    cb_sb = g_stack.enter_context(nc.sbuf_tensor("cb_sb", [128, NCC], F32))
    sw1c = g_stack.enter_context(nc.sbuf_tensor("sw1c", [128, 1], F32))
    sw2c = g_stack.enter_context(nc.sbuf_tensor("sw2c", [128, 1], F32))
    a1fac = g_stack.enter_context(nc.sbuf_tensor("a1fac", [128, 1], F32))
    a2fac = g_stack.enter_context(nc.sbuf_tensor("a2fac", [128, 1], F32))
    kcol = g_stack.enter_context(nc.sbuf_tensor("kcol_s", [128, len(KCOL_VALS)], F32))
    for _j, _v in enumerate(KCOL_VALS):
        nc.const_aps.register(F32, _v, kcol[:, _j:_j + 1])
    vdead = g_stack.enter_context(nc.sbuf_tensor("vdead", [128, 2], F32))
    scdead = g_stack.enter_context(nc.sbuf_tensor("scdead", [128, 2], F32))

    a_stack = ExitStack()
    wq_sb = a_stack.enter_context(nc.sbuf_tensor("wq_sb", [128, 8 * 1024], BF16))
    wk_sb = a_stack.enter_context(nc.sbuf_tensor("wk_sb", [128, 8 * 1024], BF16))
    wv_sb = a_stack.enter_context(nc.sbuf_tensor("wv_sb", [128, 8 * 1024], BF16))
    wg_sb = a_stack.enter_context(nc.sbuf_tensor("wg_sb", [128, 8 * 1024], BF16))
    wo_sb = a_stack.enter_context(nc.sbuf_tensor("wo_sb", [128, 8 * 1024], BF16))
    wgt_sb = a_stack.enter_context(nc.sbuf_tensor("wgt_sb", [128, 8 * 16], BF16))
    mask4 = a_stack.enter_context(nc.sbuf_tensor("mask4_s", [128, 512], BF16))
    tri = a_stack.enter_context(nc.sbuf_tensor("tri_s", [128, 128], F32))
    onesf = a_stack.enter_context(nc.sbuf_tensor("onesf_s", [128, 128], F32))
    xf = [a_stack.enter_context(nc.sbuf_tensor(f"xf{i}", [128, 1024], F32)) for i in range(2)]
    xb = [a_stack.enter_context(nc.sbuf_tensor(f"xb{i}", [128, 1024], BF16)) for i in range(2)]
    xfm = a_stack.enter_context(nc.sbuf_tensor("xfm", [128, 1024], BF16))
    q_tm = a_stack.enter_context(nc.sbuf_tensor("q_tm", [128, 1024], BF16))
    k_tm = a_stack.enter_context(nc.sbuf_tensor("k_tm", [128, 1024], BF16))
    v_tm = a_stack.enter_context(nc.sbuf_tensor("v_tm", [128, 1024], BF16))
    g_f = a_stack.enter_context(nc.sbuf_tensor("g_f", [128, 1024], F32))
    kdec = a_stack.enter_context(nc.sbuf_tensor("kdec", [128, 1024], BF16))
    kd_tm = a_stack.enter_context(nc.sbuf_tensor("kd_tm", [128, 1024], BF16))
    qfm = a_stack.enter_context(nc.sbuf_tensor("qfm", [128, 1024], BF16))
    kfm = a_stack.enter_context(nc.sbuf_tensor("kfm", [128, 1024], BF16))
    A_sb = a_stack.enter_context(nc.sbuf_tensor("A_sb", [128, 2048], BF16))
    o_tm = a_stack.enter_context(nc.sbuf_tensor("o_tm", [128, 1024], F32))
    sq_scr = a_stack.enter_context(nc.sbuf_tensor("sq_scr", [128, 1024], F32))
    sg = a_stack.enter_context(nc.sbuf_tensor("sg", [128, 1024], F32))
    gated = a_stack.enter_context(nc.sbuf_tensor("gated", [128, 1024], BF16))
    gfm = a_stack.enter_context(nc.sbuf_tensor("gfm", [128, 1024], BF16))
    y1t = a_stack.enter_context(nc.sbuf_tensor("y1t", [128, 1024], F32))
    y1o = a_stack.enter_context(nc.sbuf_tensor("y1o", [128, 1024], F32))
    S_sb = [a_stack.enter_context(nc.sbuf_tensor(f"S{i}", [128, 512], F32)) for i in range(2)]
    S_bf = [a_stack.enter_context(nc.sbuf_tensor(f"Sbf{i}", [128, 512], BF16)) for i in range(2)]
    u16 = a_stack.enter_context(nc.sbuf_tensor("u16", [128, 16], F32))
    lnw = a_stack.enter_context(nc.sbuf_tensor("lnw", [128, 16], F32))
    epos = a_stack.enter_context(nc.sbuf_tensor("epos", [128, 16], F32))
    eneg = a_stack.enter_context(nc.sbuf_tensor("eneg", [128, 16], F32))
    eB = a_stack.enter_context(nc.sbuf_tensor("eB", [128, 16], F32))
    eBmb = a_stack.enter_context(nc.sbuf_tensor("eBmb", [128, 16], F32))
    eBS = a_stack.enter_context(nc.sbuf_tensor("eBS", [128, 8], F32))
    sO = a_stack.enter_context(nc.sbuf_tensor("sO", [128, 16], F32))
    sso = a_stack.enter_context(nc.sbuf_tensor("sso", [128, 16], F32))
    ro = a_stack.enter_context(nc.sbuf_tensor("ro", [128, 16], F32))
    cols = a_stack.enter_context(nc.sbuf_tensor("cols", [128, 16], F32))
    wsl = [a_stack.enter_context(nc.sbuf_tensor(f"wsl{i}", [128, 2048], F32)) for i in range(2)]
    wqs = [a_stack.enter_context(nc.sbuf_tensor(f"wqs{i}", [128, 2048], BF16)) for i in range(2)]
    absum = a_stack.enter_context(nc.sbuf_tensor("absum", [128, 32], F32))
    scal1 = a_stack.enter_context(nc.sbuf_tensor("scal1", [1, 8], F32))

    def C(i):
        return cols[:, i:i + 1]

    ptr_free = [None]
    psa_free = [None]
    psb_free = [None]
    psm_free = [None]
    psc_gate = [None]
    psd_gate = [None]
    sbf_tok = [None, None]
    wsl_free = [None, None]
    wqs_free = [None, None]
    xf_free = {}
    xtr_done = {}
    xb_tok = {}
    xf_tok = {}
    y1_total = [0]

    for dst, src in ((ident, ident_d), (identf, identf_d), (mask4, mask4_d),
                     (tri, tri_d), (onesf, onesf_d), (onesr, onesr_d),
                     (onesc, onesc_d), (cw_sb, cw_d), (cb_sb, cb_d), (kcol, kcol_d)):
        SY(lambda e, d=dst, s=src: e.dma_start(out=d[:], in_=s).then_inc(s_w.h, 16))
        s_w.inc(16)
    for dst, src, w_ in ((wq_sb, wq_d, 1024), (wk_sb, wk_d, 1024), (wv_sb, wv_d, 1024),
                         (wg_sb, wg_d, 1024), (wo_sb, wo_d, 1024), (wgt_sb, wgt_d, 16)):
        G(lambda e, d=dst, s=src, w=w_:
          e.dma_start(out=d[:].rearrange("p (k w) -> p k w", w=w),
                      in_=s.rearrange("(k p) w -> p k w", p=128)).then_inc(s_w.h, 16))
        s_w.inc(16)
    V(lambda e: e.memset(S_sb[0][:], 0.0))
    V(lambda e: e.memset(S_sb[1][:], 0.0))
    V(lambda e: e.memset(S_bf[0][:], 0.0))
    V(lambda e: e.memset(S_bf[1][:], 0.0).then_inc(s_V.h, 1))
    vS_init = s_V.inc()

    for t_ in (lnw, u16, eB, eBmb, eBS, epos, eneg, sO, sso, ro, cols, absum,
               kdec, kd_tm, A_sb, o_tm, gated, sg, sq_scr, q_tm, k_tm, v_tm,
               g_f, qfm, kfm, gfm, xfm, y1t, vdead, scdead):
        V(lambda e, t=t_: e.memset(t[:], 0.0))
    for _ in range(4):
        SC(lambda e: e.activation(y1o[:], y1o[:], AF.Exp, bias=0.0, scale=0.0))

    wt(T, s_w.tok)
    wt(V, s_w.tok)
    wt(SC, s_w.tok)

    for ps_, nt_ in ((PSA, 8), (PSB, 8), (PSC, 4), (PSD, 4), (PSM, 4)):
        for i_ in range(nt_):
            T(lambda e, ps=ps_, i=i_: e.matmul(ps[:, i * 128:(i + 1) * 128],
                                               identf[:], identf[:],
                                               start=True, stop=True))
    for i_ in range(8):
        T(lambda e, i=i_: e.transpose(PTR[:, i * 128:(i + 1) * 128],
                                      ident[:], ident[:]))

    def mm_proj(dst_ps, w_sb):
        for n in range(2):
            for i in range(8):
                last = (n == 1 and i == 7)
                def mk(n=n, i=i, d=dst_ps, w=w_sb, last=last):
                    def f(e):
                        m = e.matmul(d[:, n * 512:(n + 1) * 512],
                                     xfm[:, i * 128:(i + 1) * 128],
                                     w[:, i * 1024 + n * 512: i * 1024 + (n + 1) * 512],
                                     start=(i == 0), stop=(i == 7))
                        if last:
                            m.then_inc(s_T.h, 1)
                    return f
                T(mk())
        return s_T.inc()

    def mm_tr8(src_sb):
        for i in range(8):
            def mk(i=i, s=src_sb):
                def f(e):
                    m = e.transpose(PTR[:, i * 128:(i + 1) * 128],
                                    s[:, i * 128:(i + 1) * 128], ident[:])
                    if i == 7:
                        m.then_inc(s_T.h, 1)
                return f
            T(mk())
        return s_T.inc()

    def head_fm(hsb, h):
        return hsb[(h % 2) * 64:(h % 2) * 64 + 64, (h // 2) * 128:(h // 2) * 128 + 128]

    def ap3(t_sb, off, dims):
        a = t_sb[:]
        return bass.AP(a.tensor, a.offset + off, dims)

    def pe_guard(dst_ps, col0):
        T(lambda e: e.matmul(dst_ps[:, col0:col0 + 128], ident[:], ident[:],
                             start=True, stop=True))
        T(lambda e: e.matmul(dst_ps[:, col0 + 128:col0 + 256], ident[:], ident[:],
                             start=True, stop=True).then_inc(s_T.h, 1))
        return s_T.inc()

    def seal_V(col, n=1, p=128):
        for _ in range(n - 1):
            V(lambda e, c=col, p=p: e.tensor_copy(vdead[0:p, 0:1], c))
        V(lambda e, c=col, p=p: e.tensor_copy(vdead[0:p, 0:1], c).then_inc(s_V.h, 1))
        return s_V.inc()

    def seal_SC(col, p=128):
        SC(lambda e, c=col, p=p: e.copy(scdead[0:p, 0:1], c).then_inc(s_S.h, 1))
        return s_S.inc()

    def pre_V(col, p=128):
        V(lambda e, c=col, p=p: e.tensor_copy(vdead[0:p, 1:2], c))

    def pre_SC(col, p=128):
        SC(lambda e, c=col, p=p: e.copy(scdead[0:p, 1:2], c))

    def emit_gt_bc(tok_gt_mm, with_eBS):
        wt(SC, tok_gt_mm)
        pre_SC(PSD[:, 0:1])
        SC(lambda e: e.activation(u16[:], PSD[:, 0:16], AF.Exp, bias=0.0, scale=-1.0))
        SC(lambda e: e.activation(lnw[:], u16[:], AF.Ln, bias=1.0, scale=1.0))
        tok_lnw = seal_SC(lnw[:, 0:1])
        wt(T, tok_lnw)
        T(lambda e: e.matmul(PSD[:, 16:32], tri[:], lnw[:], start=True, stop=True))
        T(lambda e: e.matmul(PSD[:, 32:48], onesf[:], lnw[:], start=True, stop=True))
        T(lambda e: e.matmul(PSD[:, 48:56], onesf[:],
                             ap3(lnw, 0, [[16, 128], [2, 8]]), start=True, stop=True))
        T(lambda e: e.matmul(PSD[:, 56:64], onesf[:],
                             ap3(lnw, 1, [[16, 128], [2, 8]]), start=True, stop=True)
          .then_inc(s_T.h, 1))
        s_T.inc()
        tok_bc = pe_guard(PSD, 100)
        wt(SC, tok_bc)
        pre_SC(PSD[:, 16:17])
        SC(lambda e: e.activation(epos[:], PSD[:, 16:32], AF.Exp, bias=0.0, scale=1.0))
        SC(lambda e: e.activation(eneg[:], PSD[:, 16:32], AF.Exp, bias=0.0, scale=-1.0))
        SC(lambda e: e.activation(eB[:], PSD[:, 32:48], AF.Exp, bias=0.0, scale=1.0))
        SC(lambda e: e.activation(eBS[0:64, :], PSD[0:64, 48:56], AF.Exp, bias=0.0, scale=1.0))
        SC(lambda e: e.activation(eBS[64:128, :], PSD[64:128, 56:64], AF.Exp,
                                  bias=0.0, scale=1.0))
        tok_exps = seal_SC(eBS[:, 0:1])
        return tok_exps

    def emit_rsqrt(dst, src, n, eps, bias2=0.0, inc=False):
        SC(lambda e, d=dst, s=src: e.activation(d, s, AF.Ln, bias=eps, scale=1.0 / n))
        def f(e, d=dst):
            m = e.activation(d, d, AF.Exp, bias=bias2, scale=-0.5)
            if inc:
                m.then_inc(s_S.h, 1)
        SC(f)
        return s_S.inc() if inc else None

    def emit_kv_S(scan_idx, tok_kd, tok_exps):
        wt(T, psm_free[0])
        wt(T, tok_kd)
        for h in range(H):
            def mk(h=h):
                def f(e):
                    m = e.matmul(
                        PSM[(h % 2) * 64:(h % 2) * 64 + 64, (h // 2) * 64:(h // 2) * 64 + 64],
                        kd_tm[:, h * 64:(h + 1) * 64], v_tm[:, h * 64:(h + 1) * 64],
                        start=True, stop=True)
                    if h == H - 1:
                        m.then_inc(s_T.h, 1)
                return f
            T(mk())
        s_T.inc()
        tok_kv = pe_guard(PSA, 0)
        sold, snew = S_sb[scan_idx % 2], S_sb[(scan_idx + 1) % 2]
        wt(V, tok_kv)
        wt(V, tok_exps)
        pre_V(PSM[:, 0:1])
        if dbg_scan:
            V(lambda e: e.tensor_copy(sso[:], PSD[:, 32:48]).then_inc(s_V.h, 1))
            tok_dbgc = s_V.inc()
            wt(SY, tok_dbgc)
            SY(lambda e, si=scan_idx: e.dma_start(
                out=dbg_d["Bdump"][si * 128:(si + 1) * 128, :], in_=sso[:]).then_inc(s_dbg.h, 16))
            s_dbg.inc(16)
        V(lambda e, so=sold, sn=snew: e.tensor_tensor(
            sn[:], so[:], ap3(eBS, 0, [[8, 128], [1, 8], [0, 64]]), ALU.mult))
        V(lambda e, sn=snew: e.tensor_add(sn[:], sn[:], PSM[:, 0:512]))
        V(lambda e, sn=snew, sb=S_bf[(scan_idx + 1) % 2]: e.tensor_copy(sb[:], sn[:]))
        psm_free[0] = sbf_tok[(scan_idx + 1) % 2] = seal_V(S_bf[(scan_idx + 1) % 2][:, 0:1])
        if dbg_scan:
            wt(SY, psm_free[0])
            SY(lambda e, sn=snew, si=scan_idx: e.dma_start(
                out=dbg_d["Sdump"][si * 128:(si + 1) * 128, :], in_=sn[:]).then_inc(s_dbg.h, 16))
            SY(lambda e, si=scan_idx: e.dma_start(
                out=dbg_d["Cdump"][si * 128:(si + 1) * 128, :], in_=cols[:]).then_inc(s_dbg.h, 16))
            SY(lambda e, si=scan_idx: e.dma_start(
                out=dbg_d["Edump"][si * 128:(si + 1) * 128, 0:8], in_=eBS[:]).then_inc(s_dbg.h, 16))
            SY(lambda e, si=scan_idx: e.dma_start(
                out=dbg_d["Edump"][si * 128:(si + 1) * 128, 8:24], in_=eB[:]).then_inc(s_dbg.h, 16))
            SY(lambda e, si=scan_idx: e.dma_start(
                out=dbg_d["Ldump"][si * 128:(si + 1) * 128, 0:16], in_=lnw[:]).then_inc(s_dbg.h, 16))
            SY(lambda e, si=scan_idx: e.dma_start(
                out=dbg_d["Ldump"][si * 128:(si + 1) * 128, 16:32], in_=u16[:]).then_inc(s_dbg.h, 16))
            s_dbg.inc(32)
            if scan_idx == 0:
                SY(lambda e: e.dma_start(out=dbg_d["Tdump"][:, 0:128], in_=tri[:]).then_inc(s_dbg.h, 16))
                SY(lambda e: e.dma_start(out=dbg_d["Tdump"][:, 128:256], in_=onesf[:]).then_inc(s_dbg.h, 16))
                s_dbg.inc(32)
            s_dbg.inc(64)
            wt(V, s_dbg.tok)
            wt(SC, s_dbg.tok)
        return psm_free[0]

    def slab_src(i):
        if i < 16:
            t_, hh = divmod(i, 2)
            return w1t_d[t_ * 128:(t_ + 1) * 128, hh * 2048:(hh + 1) * 2048]
        j = i - 16
        return bass.AP(w2t_d.tensor, w2t_d.offset + j * 256 * 1024,
                       [[1024, 128], [128 * 1024, 2], [1, 1024]])

    def slab_dst(i):
        if i < 16:
            t_, hh = divmod(i, 2)
            return w1q_h[t_ * 128:(t_ + 1) * 128, hh * 2048:(hh + 1) * 2048]
        j = i - 16
        return bass.AP(w2q_h.tensor, w2q_h.offset + j * 256 * 1024,
                       [[1024, 128], [128 * 1024, 2], [1, 1024]])

    def emit_q1_slab(i):
        sl = i % 2
        wt(SY, wsl_free[sl])
        SY(lambda e, i=i, sl=sl: e.dma_start(out=wsl[sl][:], in_=slab_src(i))
           .then_inc(s_sl.h, 16))
        tok_ld = s_sl.inc(16)
        wt(V, tok_ld)
        V(lambda e, i=i, sl=sl: e.tensor_reduce(absum[:, i:i + 1], wsl[sl][:], AX.X, ALU.add,
                                                apply_absolute_value=True))
        wsl_free[sl] = seal_V(absum[:, i:i + 1], 2)

    def emit_mean_math():
        wt(T, wsl_free[0])
        wt(T, wsl_free[1])
        wt(T, psm_free[0])
        T(lambda e: e.matmul(PSM[0:1, 64:96], onesc[:], absum[:], start=True, stop=True)
          .then_inc(s_T.h, 1))
        s_T.inc()
        tok_sum = pe_guard(PSM, 128)
        wt(V, tok_sum)
        pre_V(PSM[0:1, 64:65], p=1)
        V(lambda e: e.tensor_reduce(scal1[:, 0:1], PSM[0:1, 64:80], AX.X, ALU.add))
        V(lambda e: e.tensor_reduce(scal1[:, 1:2], PSM[0:1, 80:96], AX.X, ALU.add))
        wt(V, seal_V(scal1[:, 1:2], 2, p=1))
        V(lambda e: e.tensor_scalar(scal1[:, 2:3], scal1[:, 0:1], 1.0 / (DH * D), 1e-5,
                                    ALU.mult, ALU.max))
        V(lambda e: e.tensor_scalar(scal1[:, 3:4], scal1[:, 1:2], 1.0 / (DH * D), 1e-5,
                                    ALU.mult, ALU.max))
        V(lambda e: e.reciprocal(scal1[:, 4:5], scal1[:, 2:3]))
        V(lambda e: e.reciprocal(scal1[:, 5:6], scal1[:, 3:4]))
        wt(V, seal_V(scal1[:, 5:6], 2, p=1))
        V(lambda e: e.tensor_scalar(scal1[:, 6:7], scal1[:, 2:3], 1.0 / 127.0, None, ALU.mult))
        V(lambda e: e.tensor_scalar(scal1[:, 7:8], scal1[:, 3:4], 1.0 / 127.0, None, ALU.mult))
        tok_sc = seal_V(scal1[:, 7:8], p=1)
        wt(T, tok_sc)
        T(lambda e: e.matmul(PSM[:, 96:97], onesr[:], scal1[:, 4:5], start=True, stop=True))
        T(lambda e: e.matmul(PSM[:, 97:98], onesr[:], scal1[:, 5:6], start=True, stop=True))
        T(lambda e: e.matmul(PSM[:, 98:99], onesr[:], scal1[:, 6:7], start=True, stop=True))
        T(lambda e: e.matmul(PSM[:, 99:100], onesr[:], scal1[:, 7:8], start=True, stop=True)
          .then_inc(s_T.h, 1))
        s_T.inc()
        tok_bc2 = pe_guard(PSM, 128)
        wt(V, tok_bc2)
        pre_V(PSM[:, 96:97])
        V(lambda e: e.tensor_copy(sw1c[:], PSM[:, 96:97]))
        V(lambda e: e.tensor_copy(sw2c[:], PSM[:, 97:98]))
        V(lambda e: e.tensor_copy(a1fac[:], PSM[:, 98:99]))
        V(lambda e: e.tensor_copy(a2fac[:], PSM[:, 99:100]))
        psm_free[0] = seal_V(a2fac[:, 0:1])
        return psm_free[0]

    def emit_q2_slab(i, tok_scales):
        sl = i % 2
        wt(SY, wsl_free[sl])
        SY(lambda e, i=i, sl=sl: e.dma_start(out=wsl[sl][:], in_=slab_src(i))
           .then_inc(s_sl.h, 16))
        tok_ld = s_sl.inc(16)
        swc = sw1c if i < 16 else sw2c
        wt(V, tok_ld)
        wt(V, tok_scales)
        wt(V, wqs_free[sl])
        V(lambda e, sl=sl, w=swc: e.tensor_scalar(wsl[sl][:], wsl[sl][:], w[:, 0:1], MAGIC,
                                                  ALU.mult, ALU.add))
        V(lambda e, sl=sl: e.tensor_scalar(wsl[sl][:], wsl[sl][:], -MAGIC, 1.0,
                                           ALU.add, ALU.min))
        V(lambda e, sl=sl: e.tensor_scalar(wqs[sl][:], wsl[sl][:], -1.0, None, ALU.max))
        wsl_free[sl] = seal_V(wqs[sl][:, 0:1])
        wt(SY, wsl_free[sl])
        SY(lambda e, i=i, sl=sl: e.dma_start(out=slab_dst(i), in_=wqs[sl][:])
           .then_inc(s_qw.h, 16))
        wqs_free[sl] = s_qw.inc(16)

    q_state = {"q1": 0, "mean": False, "q2": 0, "scales": None}

    def interleave_quant():
        if q_state["q1"] < 32:
            emit_q1_slab(q_state["q1"])
            emit_q1_slab(q_state["q1"] + 1)
            q_state["q1"] += 2
        elif not q_state["mean"]:
            q_state["scales"] = emit_mean_math()
            q_state["mean"] = True
        elif q_state["q2"] < 32:
            emit_q2_slab(q_state["q2"], q_state["scales"])
            emit_q2_slab(q_state["q2"] + 1, q_state["scales"])
            q_state["q2"] += 2

    def emit_prefix_chunk(p, scan_idx):
        sl = p % 2
        wt(T, xb_tok[("pre", p)])
        wt(T, ptr_free[0])
        tok_xtr = mm_tr8(xb[sl])
        xtr_done[("pre", p)] = tok_xtr
        wt(V, tok_xtr)
        pre_V(PTR[:, 0:1])
        V(lambda e: e.tensor_copy(xfm[:], PTR[:]))
        ptr_free[0] = tok_xfm = seal_V(xfm[:, 0:1])
        wt(T, tok_xfm)
        wt(T, psa_free[0])
        tok_k = mm_proj(PSA, wk_sb)
        wt(T, psb_free[0])
        tok_v = mm_proj(PSB, wv_sb)
        wt(T, psd_gate[0])
        for i in range(8):
            def mk(i=i):
                def f(e):
                    m = e.matmul(PSD[:, 0:16], xfm[:, i * 128:(i + 1) * 128],
                                 wgt_sb[:, i * 16:(i + 1) * 16],
                                 start=(i == 0), stop=(i == 7))
                    if i == 7:
                        m.then_inc(s_T.h, 1)
                return f
            T(mk())
        s_T.inc()
        tok_gt = pe_guard(PSD, 100)
        wt(SC, tok_k)
        pre_SC(PSA[:, 0:1])
        SC(lambda e: e.copy(k_tm[:], PSA[:]))
        psa_free[0] = seal_SC(k_tm[:, 0:1])
        SC(lambda e: e.activation(sq_scr[:], k_tm[:], AF.Square, accum_out=C(1)))
        wt(V, tok_v)
        pre_V(PSB[:, 0:1])
        V(lambda e: e.tensor_copy(v_tm[:], PSB[:]))
        psb_free[0] = tok_vtm = seal_V(v_tm[:, 0:1])
        wt(SC, tok_vtm)
        SC(lambda e: e.activation(sq_scr[:], v_tm[:], AF.Square, accum_out=C(2)))
        tok_exps = emit_gt_bc(tok_gt, True)
        psd_gate[0] = tok_exps
        emit_rsqrt(C(4), C(1), 1024.0, 1e-6)
        emit_rsqrt(C(5), C(2), 1024.0, 1e-6)
        SC(lambda e: e.activation(C(6), C(4), AF.Identity, bias=0.0, scale=C(5)))
        tok_rkv = seal_SC(C(6))
        wt(V, tok_rkv)
        V(lambda e: e.scalar_tensor_tensor(
            kdec[:], k_tm[:], C(6), ap3(eneg, 0, [[16, 128], [1, 16], [0, 64]]),
            ALU.mult, ALU.mult))
        V(lambda e: e.tensor_tensor(
            kd_tm[:], kdec[:], ap3(eB, 0, [[16, 128], [1, 16], [0, 64]]), ALU.mult))
        tok_kd = seal_V(kd_tm[:, 0:1])
        emit_kv_S(scan_idx, tok_kd, tok_exps)

    def emit_attn_chunk(c, scan_idx):
        sl = c % 2
        wt(T, xb_tok[("attn", c)])
        wt(T, ptr_free[0])
        tok_xtr = mm_tr8(xb[sl])
        xtr_done[("attn", c)] = tok_xtr
        wt(V, tok_xtr)
        pre_V(PTR[:, 0:1])
        V(lambda e: e.tensor_copy(xfm[:], PTR[:]))
        ptr_free[0] = tok_xfm = seal_V(xfm[:, 0:1])
        wt(T, tok_xfm)
        wt(T, psa_free[0])
        tok_q = mm_proj(PSA, wq_sb)
        wt(T, psb_free[0])
        tok_k = mm_proj(PSB, wk_sb)
        wt(SC, tok_q)
        pre_SC(PSA[:, 0:1])
        SC(lambda e: e.copy(q_tm[:], PSA[:]))
        tok_qtm = seal_SC(q_tm[:, 0:1])
        SC(lambda e: e.activation(sq_scr[:], q_tm[:], AF.Square, accum_out=C(0)))
        wt(V, tok_k)
        pre_V(PSB[:, 0:1])
        V(lambda e: e.tensor_copy(k_tm[:], PSB[:]))
        tok_ktm = seal_V(k_tm[:, 0:1])
        wt(SC, tok_ktm)
        SC(lambda e: e.activation(sq_scr[:], k_tm[:], AF.Square, accum_out=C(1))
           .then_inc(s_S.h, 1))
        s_S.inc()
        wt(T, psd_gate[0])
        for i in range(8):
            def mk(i=i):
                def f(e):
                    m = e.matmul(PSD[:, 0:16], xfm[:, i * 128:(i + 1) * 128],
                                 wgt_sb[:, i * 16:(i + 1) * 16],
                                 start=(i == 0), stop=(i == 7))
                    if i == 7:
                        m.then_inc(s_T.h, 1)
                return f
            T(mk())
        s_T.inc()
        tok_gt = pe_guard(PSD, 100)
        wt(SC, tok_gt)
        pre_SC(PSD[:, 0:1])
        SC(lambda e: e.activation(u16[:], PSD[:, 0:16], AF.Exp, bias=0.0, scale=-1.0))
        SC(lambda e: e.activation(lnw[:], u16[:], AF.Ln, bias=1.0, scale=1.0))
        tok_lnw = seal_SC(lnw[:, 0:1])
        wt(T, tok_qtm)
        tok_v = mm_proj(PSA, wv_sb)
        wt(T, tok_ktm)
        tok_g = mm_proj(PSB, wg_sb)
        wt(T, tok_lnw)
        T(lambda e: e.matmul(PSD[:, 16:32], tri[:], lnw[:], start=True, stop=True))
        T(lambda e: e.matmul(PSD[:, 32:48], onesf[:], lnw[:], start=True, stop=True))
        T(lambda e: e.matmul(PSD[:, 48:56], onesf[:],
                             ap3(lnw, 0, [[16, 128], [2, 8]]), start=True, stop=True))
        T(lambda e: e.matmul(PSD[:, 56:64], onesf[:],
                             ap3(lnw, 1, [[16, 128], [2, 8]]), start=True, stop=True)
          .then_inc(s_T.h, 1))
        s_T.inc()
        tok_bc = pe_guard(PSD, 100)
        wt(SC, tok_bc)
        pre_SC(PSD[:, 16:17])
        SC(lambda e: e.activation(epos[:], PSD[:, 16:32], AF.Exp, bias=0.0, scale=1.0))
        SC(lambda e: e.activation(eneg[:], PSD[:, 16:32], AF.Exp, bias=0.0, scale=-1.0))
        SC(lambda e: e.activation(eB[:], PSD[:, 32:48], AF.Exp, bias=0.0, scale=1.0))
        SC(lambda e: e.activation(eBS[0:64, :], PSD[0:64, 48:56], AF.Exp, bias=0.0, scale=1.0))
        SC(lambda e: e.activation(eBS[64:128, :], PSD[64:128, 56:64], AF.Exp,
                                  bias=0.0, scale=1.0))
        tok_exps = seal_SC(eBS[:, 0:1])
        wt(V, tok_v)
        pre_V(PSA[:, 0:1])
        V(lambda e: e.tensor_copy(v_tm[:], PSA[:]))
        psa_free[0] = tok_vtm = seal_V(v_tm[:, 0:1])
        wt(V, tok_exps)
        V(lambda e: e.tensor_tensor(
            kdec[:], k_tm[:], ap3(eneg, 0, [[16, 128], [1, 16], [0, 64]]), ALU.mult))
        tok_kdec = seal_V(kdec[:, 0:1])
        wt(SC, tok_vtm)
        SC(lambda e: e.activation(sq_scr[:], v_tm[:], AF.Square, accum_out=C(2)))
        wt(SC, tok_g)
        pre_SC(PSB[:, 0:1])
        SC(lambda e: e.copy(g_f[:], PSB[:]))
        psb_free[0] = seal_SC(g_f[:, 0:1])
        emit_rsqrt(C(3), C(0), 1024.0, 1e-6, bias2=LN8INV)
        emit_rsqrt(C(4), C(1), 1024.0, 1e-6)
        emit_rsqrt(C(5), C(2), 1024.0, 1e-6)
        SC(lambda e: e.activation(C(6), C(4), AF.Identity, bias=0.0, scale=C(5)))
        tok_rkv = seal_SC(C(6))
        wt(V, tok_rkv)
        V(lambda e: e.scalar_tensor_tensor(
            kd_tm[:], kdec[:], C(6), ap3(eB, 0, [[16, 128], [1, 16], [0, 64]]),
            ALU.mult, ALU.mult))
        V(lambda e: e.tensor_scalar(sO[:], epos[:], C(3), None, ALU.mult))
        tok_kd = seal_V(kd_tm[:, 0:1])
        wt(T, ptr_free[0])
        wt(T, tok_qtm)
        tok_qtr = mm_tr8(q_tm)
        wt(V, tok_qtr)
        pre_V(PTR[:, 0:1])
        V(lambda e: e.tensor_copy(qfm[:], PTR[:]))
        ptr_free[0] = seal_V(qfm[:, 0:1])
        wt(T, ptr_free[0])
        wt(T, tok_kdec)
        tok_ktr = mm_tr8(kdec)
        wt(V, tok_ktr)
        pre_V(PTR[:, 0:1])
        V(lambda e: e.tensor_copy(kfm[:], PTR[:]))
        ptr_free[0] = tok_kfm = seal_V(kfm[:, 0:1])
        emit_kv_S(scan_idx, tok_kd, tok_exps)
        wt(T, tok_kfm)
        wt(T, psc_gate[0])
        wt(T, tok_exps)
        evac_toks = []
        for r in range(2):
            heads_e = [8 * r + 0, 8 * r + 2, 8 * r + 4, 8 * r + 6]
            heads_o = [h + 1 for h in heads_e]
            if r == 1:
                wt(T, evac_toks[0])
                wt(T, evac_toks[1])
            for si, h in enumerate(heads_e):
                def mk(si=si, h=h):
                    def f(e):
                        m = e.matmul(PSC[:, si * 128:(si + 1) * 128],
                                     head_fm(kfm, h), head_fm(qfm, h), start=True, stop=True)
                        if si == 3:
                            m.then_inc(s_T.h, 1)
                    return f
                T(mk())
            tok_Ae = s_T.inc()
            for si, h in enumerate(heads_o):
                def mk(si=si, h=h):
                    def f(e):
                        m = e.matmul(PSD[:, si * 128:(si + 1) * 128],
                                     head_fm(kfm, h), head_fm(qfm, h), start=True, stop=True)
                        if si == 3:
                            m.then_inc(s_T.h, 1)
                    return f
                T(mk())
            tok_Ao = s_T.inc()
            wt(V, tok_Ae)
            h0 = heads_e[0]
            pre_V(PSC[:, 0:1])
            V(lambda e, h0=h0: e.scalar_tensor_tensor(
                ap3(A_sb, h0 * 128, [[2048, 128], [256, 4], [1, 128]]),
                PSC[:].rearrange("p (s c) -> p s c", c=128), C(6),
                mask4[:].rearrange("p (s c) -> p s c", c=128),
                ALU.mult, ALU.mult))
            evac_toks.append(seal_V(A_sb[:, h0 * 128:h0 * 128 + 1]))
            wt(V, tok_Ao)
            h1 = heads_o[0]
            pre_V(PSD[:, 0:1])
            V(lambda e, h1=h1: e.scalar_tensor_tensor(
                ap3(A_sb, h1 * 128, [[2048, 128], [256, 4], [1, 128]]),
                PSD[:].rearrange("p (s c) -> p s c", c=128), C(6),
                mask4[:].rearrange("p (s c) -> p s c", c=128),
                ALU.mult, ALU.mult))
            evac_toks.append(seal_V(A_sb[:, h1 * 128:h1 * 128 + 1]))
        tok_A = evac_toks[-1]
        wt(T, tok_A)
        wt(T, sbf_tok[scan_idx % 2])
        sold = S_bf[scan_idx % 2]
        for h in range(H):
            bank = PSC if h % 2 == 0 else PSD
            off = (h // 2) * 64
            T(lambda e, h=h, off=off, b=bank: e.matmul(
                b[:, off:off + 64], A_sb[:, h * 128:(h + 1) * 128],
                v_tm[:, h * 64:(h + 1) * 64], start=True, stop=False))
            def mk(h=h, off=off, so=sold, b=bank):
                def f(e):
                    m = e.matmul(b[:, off:off + 64], head_fm(qfm, h),
                                 so[(h % 2) * 64:(h % 2) * 64 + 64,
                                    (h // 2) * 64:(h // 2) * 64 + 64],
                                 start=False, stop=True)
                    if h == H - 1:
                        m.then_inc(s_T.h, 1)
                return f
            T(mk())
        s_T.inc()
        tok_o = pe_guard(PSA, 0)
        wt(V, tok_o)
        pre_V(PSC[:, 0:1])
        pre_V(PSD[:, 0:1])
        V(lambda e: e.tensor_tensor(
            ap3(o_tm, 0, [[1024, 128], [128, 8], [1, 64]]), PSC[:, 0:512],
            ap3(sO, 0, [[16, 128], [2, 8], [0, 64]]), ALU.mult))
        V(lambda e: e.tensor_tensor(
            ap3(o_tm, 64, [[1024, 128], [128, 8], [1, 64]]), PSD[:, 0:512],
            ap3(sO, 1, [[16, 128], [2, 8], [0, 64]]), ALU.mult))
        tok_oev = seal_V(o_tm[:, 0:1])
        wt(SC, tok_oev)
        SC(lambda e: e.activation(sq_scr[:], o_tm[:], AF.Square))
        tok_o2 = seal_SC(sq_scr[:, 0:1])
        wt(V, tok_o2)
        V(lambda e: e.tensor_reduce(sso[:], sq_scr[:].rearrange("p (h e) -> p h e", e=64),
                                    AX.X, ALU.add))
        tok_sso = seal_V(sso[:, 0:1], 2)
        wt(SC, tok_sso)
        emit_rsqrt(ro[:], sso[:], 64.0, 1e-6)
        SC(lambda e: e.activation(sg[:], g_f[:], AF.Silu))
        tok_sg = seal_SC(sg[:, 0:1])
        wt(V, tok_sg)
        V(lambda e: e.tensor_tensor(sq_scr[:], o_tm[:],
                                    ap3(ro, 0, [[16, 128], [1, 16], [0, 64]]), ALU.mult))
        V(lambda e: e.tensor_mul(gated[:], sq_scr[:], sg[:]))
        tok_gated = seal_V(gated[:, 0:1])
        wt(T, ptr_free[0])
        wt(T, tok_gated)
        tok_gtr = mm_tr8(gated)
        wt(V, tok_gtr)
        pre_V(PTR[:, 0:1])
        V(lambda e: e.tensor_copy(gfm[:], PTR[:]))
        ptr_free[0] = tok_gfm = seal_V(gfm[:, 0:1])
        wt(T, tok_gfm)
        for n in range(2):
            bank = PSC if n == 0 else PSD
            for i in range(8):
                def mk(n=n, i=i, b=bank):
                    def f(e):
                        m = e.matmul(b[:, 0:512], gfm[:, i * 128:(i + 1) * 128],
                                     wo_sb[:, i * 1024 + n * 512: i * 1024 + (n + 1) * 512],
                                     start=(i == 0), stop=(i == 7))
                        if n == 1 and i == 7:
                            m.then_inc(s_T.h, 1)
                    return f
                T(mk())
        tok_wo = s_T.inc()
        wt(V, tok_wo)
        wt(V, xf_tok[c])
        pre_V(PSC[:, 0:1])
        pre_V(PSD[:, 0:1])
        V(lambda e, sl=sl: e.scalar_tensor_tensor(
            y1t[:, 0:512], PSC[:, 0:512], 1.0, xf[sl][:, 0:512], ALU.mult, ALU.add))
        V(lambda e, sl=sl: e.scalar_tensor_tensor(
            y1t[:, 512:1024], PSD[:, 0:512], 1.0, xf[sl][:, 512:1024], ALU.mult, ALU.add))
        V(lambda e: e.tensor_reduce(C(9), y1t[:], AX.X, ALU.add))
        tok_resid = seal_V(C(9), 2)
        psc_gate[0] = tok_resid
        psd_gate[0] = tok_resid
        xf_free[c] = tok_resid
        wt(SC, tok_resid)
        SC(lambda e: e.activation(sq_scr[:], y1t[:], AF.Square, accum_out=C(14)))
        SC(lambda e: e.activation(C(10), C(14), AF.Identity, bias=0.0, scale=1.0 / 1024.0))
        SC(lambda e: e.activation(C(11), C(9), AF.Square, bias=0.0, scale=1.0 / 1024.0))
        tok_msq = seal_SC(C(11))
        wt(V, tok_msq)
        V(lambda e: e.tensor_sub(C(11), C(10), C(11)))
        tok_var = seal_V(C(11))
        wt(SC, tok_var)
        SC(lambda e: e.activation(C(12), C(11), AF.Ln, bias=1e-5, scale=1.0))
        SC(lambda e: e.activation(C(12), C(12), AF.Exp, bias=0.0, scale=-0.5))
        SC(lambda e: e.activation(C(13), C(9), AF.Identity, bias=0.0, scale=-1.0 / 1024.0))
        SC(lambda e: e.activation(C(13), C(13), AF.Identity, bias=0.0, scale=C(12)))
        wt(SC, (s_y1, s_y1.n))
        SC(lambda e: e.activation(y1o[:], y1t[:], AF.Identity, bias=C(13), scale=C(12)))
        tok_y1 = seal_SC(y1o[:, 0:1])
        wt(SY, tok_y1)
        SY(lambda e, c=c: e.dma_start(out=y1buf[c * 128:(c + 1) * 128, :], in_=y1o[:])
           .then_inc(s_y1.h, 16))

        s_y1.inc(16)
        if dbg_chunk is not None and c == dbg_chunk:
            bufs = dict(q_tm=q_tm, k_tm=k_tm, v_tm=v_tm, g_f=g_f, lnw=lnw, epos=epos,
                        eneg=eneg, eB=eB, eBmb=eBmb, kdec=kdec, kd_tm=kd_tm, qfm=qfm,
                        kfm=kfm, A_sb=A_sb, o_tm=o_tm, ro=ro, sg=sg, gated=gated,
                        y1t=y1t, cols=cols, sO=sO, sso=sso,
                        S_old=S_bf[scan_idx % 2], S_new=S_sb[(scan_idx + 1) % 2])
            for nm, buf in bufs.items():
                SY(lambda e, d=dbg_d[nm], b=buf: e.dma_start(out=d, in_=b[:])
                   .then_inc(s_dbg.h, 16))
                s_dbg.inc(16)
            wt(T, s_dbg.tok)
            wt(V, s_dbg.tok)
            wt(SC, s_dbg.tok)

    for p in range(2):
        G(lambda e, p=p: e.dma_start(out=xb[p % 2][:], in_=x_pre[p * 128:(p + 1) * 128, :])
          .then_inc(s_xb.h, 16))
        xb_tok[("pre", p)] = s_xb.inc(16)

    scan_idx = 0
    for p in range(N_PRE):
        emit_prefix_chunk(p, scan_idx)
        ser_barrier()
        interleave_quant()
        scan_idx += 1
        nxt = p + 2
        if nxt < N_PRE:
            wt(G, xtr_done[("pre", p)])
            G(lambda e, n=nxt: e.dma_start(out=xb[n % 2][:], in_=x_pre[n * 128:(n + 1) * 128, :])
              .then_inc(s_xb.h, 16))
            xb_tok[("pre", nxt)] = s_xb.inc(16)

    for c in range(2):
        SY(lambda e, c=c: e.dma_start(out=xf[c % 2][:], in_=x_attn[c * 128:(c + 1) * 128, :])
           .then_inc(s_x.h, 16))
        xf_tok[c] = s_x.inc(16)
        wt(G, xtr_done[("pre", N_PRE - 1 - c)])
        G(lambda e, c=c: e.dma_start(out=xb[c % 2][:], in_=x_attn[c * 128:(c + 1) * 128, :])
          .then_inc(s_xb.h, 16))
        xb_tok[("attn", c)] = s_xb.inc(16)

    for c in range(N_ATTN):
        emit_attn_chunk(c, scan_idx)
        ser_barrier()
        interleave_quant()
        scan_idx += 1
        nxt = c + 2
        if nxt < N_ATTN:
            wt(SY, xf_free[c])
            SY(lambda e, n=nxt: e.dma_start(out=xf[n % 2][:], in_=x_attn[n * 128:(n + 1) * 128, :])
               .then_inc(s_x.h, 16))
            xf_tok[nxt] = s_x.inc(16)
            wt(G, xtr_done[("attn", c)])
            G(lambda e, n=nxt: e.dma_start(out=xb[n % 2][:], in_=x_attn[n * 128:(n + 1) * 128, :])
              .then_inc(s_xb.h, 16))
            xb_tok[("attn", nxt)] = s_xb.inc(16)

    y1_total[0] = s_y1.n
    assert q_state["q2"] >= 32, "quant pipeline incomplete"

    wt(T, psm_free[0])
    T(lambda e: e.matmul(PSM[:, 100:101], onesr[:], onesc[0:1, 0:1], start=True, stop=True)
      .then_inc(s_bar.h, 1))
    V(lambda e: e.memset(scal1[:, 0:1], 0.0).then_inc(s_bar.h, 1))
    SC(lambda e: e.memzero(u16[:, 0:1]).then_inc(s_bar.h, 1))
    G(lambda e: e.engine_nop().then_inc(s_bar.h, 1))
    s_bar.inc(4)
    BAR = s_bar.tok
    Y1ALL = s_y1.tok
    QWALL = s_qw.tok
    XBALL = s_xb.tok
    XALL = s_x.tok
    SLALL = s_sl.tok

    a_stack.close()

    b_stack = ExitStack()
    w1q_sb = b_stack.enter_context(nc.sbuf_tensor("w1q_sb", [128, 8 * DH], BF16))
    w2q_sb = b_stack.enter_context(nc.sbuf_tensor("w2q_sb", [128, NCC * 1024], BF16))
    ext = [b_stack.enter_context(nc.sbuf_tensor(f"ext{i}", [128, NCC * 258], BF16))
           for i in range(2)]
    convb = b_stack.enter_context(nc.sbuf_tensor("convb", [128, NCC * 256], BF16))
    y1r = [b_stack.enter_context(nc.sbuf_tensor(f"y1r{i}", [128, 2048], F32)) for i in range(2)]
    ytr1 = b_stack.enter_context(nc.sbuf_tensor("ytr1", [128, 512], F32))
    ytf = b_stack.enter_context(nc.sbuf_tensor("ytf", [128, 512], BF16))
    ytfm = b_stack.enter_context(nc.sbuf_tensor("ytfm", [128, 2048], BF16))
    rscr = b_stack.enter_context(nc.sbuf_tensor("rscr", [128, 512], F32))
    bcols = b_stack.enter_context(nc.sbuf_tensor("bcols", [128, 16], F32))
    y2pre = b_stack.enter_context(nc.sbuf_tensor("y2pre", [128, 1024], F32))
    mxc = y2pre

    def BC(i):
        return bcols[:, i:i + 1]

    for st in (T, V, SC, G):
        wt(st, BAR)
    wt(SY, BAR)
    for st in (T, V, SC):
        wt(st, Y1ALL)
        wt(st, QWALL)
        wt(st, XBALL)
        wt(st, XALL)
        wt(st, SLALL)
    wt(SY, QWALL)
    SY(lambda e: e.dma_start(out=w1q_sb[:].rearrange("p (k w) -> p k w", w=DH),
                             in_=w1q_h.rearrange("(k p) w -> p k w", p=128))
       .then_inc(s_qw.h, 16))
    s_qw.inc(16)
    SY(lambda e: e.dma_start(out=w2q_sb[:].rearrange("p (k w) -> p k w", w=1024),
                             in_=w2q_h.rearrange("(k p) w -> p k w", p=128))
       .then_inc(s_qw.h, 16))
    tok_wqld = s_qw.inc(16)
    wt(T, tok_wqld)

    hq_free = {"A": None, "B": None}
    y1r_free = [None, None]
    y1r_free_sc = [None, None]
    y1r_tok = [None, None]
    pscd_free = [None]
    psm_b_free = [None]
    out_tok = [None]
    ext_sources = {}

    def emit_ytile(w, i):
        ysl = y1r[w % 2]
        wt(V, y1r_tok[w % 2])
        wt(V, s2_sc_tok[0])
        if out_tok[0] is not None:
            wt(V, out_tok[0])
        V(lambda e, i=i: e.tensor_reduce(BC(i), ysl[:, i * 1024:(i + 1) * 1024], AX.X,
                                         ALU.max, apply_absolute_value=True))
        tok_mx1 = seal_V(BC(i), 2)
        wt(SC, tok_mx1)
        SC(lambda e, i=i: e.activation(BC(2 + i), BC(0 + i), AF.Relu, bias=-1e-5, scale=1.0))
        SC(lambda e, i=i: e.activation(BC(2 + i), BC(2 + i), AF.Ln, bias=1e-5, scale=1.0))
        SC(lambda e, i=i: e.activation(BC(4 + i), BC(2 + i), AF.Exp, bias=4.8441870864585871,
                                       scale=-1.0))
        SC(lambda e, i=i: e.activation(BC(0 + i), BC(2 + i), AF.Exp, bias=0.0, scale=1.0))
        SC(lambda e, i=i: e.activation(BC(6 + i), BC(0 + i), AF.Identity, bias=0.0,
                                       scale=a1fac[:]))
        tok_cols = seal_SC(BC(6 + i))
        prev_tr = None
        for hh in range(2):
            if hh == 1:
                wt(SC, tok_ytf)
            def mkr(i=i, hh=hh):
                def f(e):
                    e.activation(ytr1[:], ysl[:, i * 1024 + hh * 512: i * 1024 + (hh + 1) * 512],
                                 AF.Identity, bias=MAGIC, scale=BC(4 + i))
                return f
            SC(mkr())
            tok_r1 = seal_SC(ytr1[:, 0:1])
            wt(V, tok_r1)
            if prev_tr is not None:
                wt(V, prev_tr)
            V(lambda e, i=i: e.tensor_scalar(ytf[:], ytr1[:], -MAGIC, BC(6 + i),
                                             ALU.add, ALU.mult))
            tok_ytf = seal_V(ytf[:, 0:1])
            if hh == 0:
                wt(T, ptr_free[0])
            wt(T, tok_ytf)
            for j in range(4):
                def mkt(j=j, hh=hh):
                    def f(e):
                        m = e.transpose(PTR[:, (hh * 4 + j) * 128:(hh * 4 + j + 1) * 128],
                                        ytf[:, j * 128:(j + 1) * 128], ident[:])
                        if j == 3:
                            m.then_inc(s_T.h, 1)
                    return f
                T(mkt())
            prev_tr = s_T.inc()
        wt(V, prev_tr)
        pre_V(PTR[:, 1023:1024])
        for k in range(8):
            def mkc(k=k, i=i):
                def f(e):
                    e.tensor_copy(ytfm[:, k * 256 + i * 128: k * 256 + (i + 1) * 128],
                                  PTR[:, k * 128:(k + 1) * 128])
                return f
            V(mkc())
        ptr_free[0] = seal_V(ytfm[:, i * 128:i * 128 + 1])
        return ptr_free[0]

    def emit_hwindow(w):
        wt(SY, y1r_free[w % 2])
        wt(SY, y1r_free_sc[w % 2])
        SY(lambda e, w=w: e.dma_start(
            out=y1r[w % 2][:].rearrange("p (i j) -> p i j", j=1024),
            in_=y1buf[w * 256:(w + 1) * 256, :].rearrange("(i p) j -> p i j", p=128))
           .then_inc(s_y1r.h, 16))
        y1r_tok[w % 2] = s_y1r.inc(16)
        tok_fm = None
        for i in range(2):
            tok_fm = emit_ytile(w, i)
        if dbg_win is not None and w == dbg_win:
            wt(SY, tok_fm)
            SY(lambda e: e.dma_start(out=dbg_d["ytfm"], in_=ytfm[:]).then_inc(s_dbg.h, 16))
            s_dbg.inc(16)
            wt(T, s_dbg.tok)
        wt(T, tok_fm)
        for quad in range(8):
            bank = PSA if quad % 2 == 0 else PSB
            bname = "A" if quad % 2 == 0 else "B"
            wt(T, hq_free[bname])
            for j in range(4):
                cc = quad * 4 + j
                for k in range(8):
                    def mk(cc=cc, j=j, k=k, b=bank, last=(j == 3 and k == 7)):
                        def f(e):
                            m = e.matmul(b[:, j * 256:(j + 1) * 256],
                                         w1q_sb[:, k * DH + cc * 128: k * DH + (cc + 1) * 128],
                                         ytfm[:, k * 256:(k + 1) * 256],
                                         start=(k == 0), stop=(k == 7))
                            if last:
                                m.then_inc(s_T.h, 1)
                        return f
                    T(mk())
            tok_quad = s_T.inc()
            wt(V, tok_quad)
            q0 = quad * 4
            pre_V(bank[:, 0:1])
            if w > 0:
                V(lambda e, q0=q0, b=bank, w=w: e.tensor_copy(
                    ext[(w - 1) % 2][:].rearrange("p (g c) -> p g c", c=258)[:, q0:q0 + 4, 129:258],
                    b[:].rearrange("p (g c) -> p g c", c=256)[:, :, 0:129]))
            def mk2(q0=q0, b=bank, w=w):
                def f(e):
                    e.tensor_copy(
                        ext[w % 2][:].rearrange("p (g c) -> p g c", c=258)[:, q0:q0 + 4, 0:129],
                        b[:].rearrange("p (g c) -> p g c", c=256)[:, :, 127:256])
                return f
            V(mk2())
            hq_free[bname] = seal_V(ext[w % 2][:, q0 * 258:q0 * 258 + 1])
            if w >= 1:
                emit_conv_quad(w - 1, quad, hq_free[bname])
        ext_sources[w] = hq_free["B"]

    head_tok = {}
    w2_tok = {}
    conv_done = {}
    s2_sc_tok = [None]

    def emit_conv_quad(o, q, tok_evac):
        exto = ext[o % 2]
        wt(SC, tok_evac)
        if q == 0 and (o - 1) in w2_tok:
            wt(SC, w2_tok[o - 1])
        for j in range(4):
            cc = q * 4 + j
            def mkA(cc=cc, j=j):
                def f(e):
                    e.activation(convb[:, cc * 256:(cc + 1) * 256],
                                 exto[:, cc * 258 + 1: cc * 258 + 257],
                                 AF.Identity, bias=cb_sb[:, cc:cc + 1],
                                 scale=cw_sb[:, 3 * cc + 1:3 * cc + 2])
                return f
            SC(mkA())
        tok_cA = seal_SC(convb[:, q * 4 * 256: q * 4 * 256 + 1])
        wt(V, tok_cA)
        for j in range(4):
            cc = q * 4 + j
            V(lambda e, cc=cc: e.scalar_tensor_tensor(
                convb[:, cc * 256:(cc + 1) * 256], exto[:, cc * 258: cc * 258 + 256],
                cw_sb[:, 3 * cc:3 * cc + 1], convb[:, cc * 256:(cc + 1) * 256],
                ALU.mult, ALU.add))
        for j in range(4):
            cc = q * 4 + j
            def mkC(cc=cc, j=j):
                def f(e):
                    e.scalar_tensor_tensor(
                        convb[:, cc * 256:(cc + 1) * 256], exto[:, cc * 258 + 2: cc * 258 + 258],
                        cw_sb[:, 3 * cc + 2:3 * cc + 3], convb[:, cc * 256:(cc + 1) * 256],
                        ALU.mult, ALU.add)
                return f
            V(mkC())
        conv_done[o] = seal_V(convb[:, q * 4 * 256: q * 4 * 256 + 1])

    def emit_out_head(o):
        exto = ext[o % 2]
        wt(SC, conv_done[o])
        SC(lambda e: e.activation(convb[:], convb[:], AF.Silu))
        tok_silu = seal_SC(convb[:, 0:1])
        if dbg_win is not None and o == dbg_win:
            wt(SY, tok_silu)
            SY(lambda e: e.dma_start(out=dbg_d["silu"], in_=convb[:]).then_inc(s_dbg.h, 16))
            s_dbg.inc(16)
            wt(V, s_dbg.tok)
        wt(V, tok_silu)
        if out_tok[0] is not None:
            wt(V, out_tok[0])
        V(lambda e: e.tensor_reduce(mxc[:, 0:256],
                                    ap3(convb, 0, [[NCC * 256, 128], [1, 256], [256, NCC]]),
                                    AX.X, ALU.max, apply_absolute_value=True))
        head_tok[o] = seal_V(mxc[:, 0:1], 2)

    def emit_out_tail(o):
        exto = ext[o % 2]
        tok_mx = head_tok[o]
        wt(T, tok_mx)
        wt(T, psm_b_free[0])
        T(lambda e: e.matmul(PSM[:, 0:128], mxc[:, 0:128], identf[:], start=True, stop=True))
        T(lambda e: e.matmul(PSM[:, 128:256], mxc[:, 128:256], identf[:], start=True, stop=True)
          .then_inc(s_T.h, 1))
        s_T.inc()
        tok_mxT = pe_guard(PSM, 256)
        wt(V, tok_mxT)
        pre_V(PSM[:, 0:1])
        V(lambda e: e.tensor_reduce(BC(0), PSM[:, 0:128], AX.X, ALU.max))
        V(lambda e: e.tensor_reduce(BC(1), PSM[:, 128:256], AX.X, ALU.max))
        tok_mx2 = seal_V(BC(1), 2)
        wt(SC, tok_mx2)
        for i in range(2):
            SC(lambda e, i=i: e.activation(BC(2 + i), BC(0 + i), AF.Relu, bias=-1e-5, scale=1.0))
            SC(lambda e, i=i: e.activation(BC(2 + i), BC(2 + i), AF.Ln, bias=1e-5, scale=1.0))
            SC(lambda e, i=i: e.activation(BC(4 + i), BC(2 + i), AF.Exp, bias=4.8441870864585871,
                                           scale=-1.0))
            SC(lambda e, i=i: e.activation(BC(0 + i), BC(2 + i), AF.Exp, bias=0.0, scale=1.0))
            def mk(i=i):
                def f(e):
                    e.activation(BC(8 + i), BC(0 + i), AF.Identity, bias=0.0,
                                 scale=a2fac[:])
                return f
            SC(mk())
        tok_s2 = seal_SC(BC(5))
        s2_sc_tok[0] = tok_s2
        wt(T, tok_s2)
        T(lambda e: e.matmul(PSM[0:1, 256:384], BC(4), identf[:], start=True, stop=True))
        T(lambda e: e.matmul(PSM[0:1, 384:512], BC(5), identf[:], start=True, stop=True)
          .then_inc(s_T.h, 1))
        s_T.inc()
        tok_row = pe_guard(PSM, 0)
        wt(V, tok_row)
        pre_V(PSM[0:1, 256:257], p=1)
        V(lambda e: e.tensor_copy(y2pre[0:1, 256:384], PSM[0:1, 256:384]))
        V(lambda e: e.tensor_copy(y2pre[0:1, 384:512], PSM[0:1, 384:512]))
        tok_rowc = seal_V(y2pre[0:1, 256:257], p=1)
        wt(T, tok_rowc)
        T(lambda e: e.matmul(PSM[:, 0:256], onesr[:], y2pre[0:1, 256:512],
                             start=True, stop=True).then_inc(s_T.h, 1))
        s_T.inc()
        tok_s2bm = pe_guard(PSM, 256)
        wt(V, tok_s2bm)
        pre_V(PSM[:, 0:1])
        V(lambda e: e.tensor_copy(y2pre[:, 256:512], PSM[:, 0:256]))
        psm_b_free[0] = seal_V(y2pre[:, 256:257])
        wt(V, psm_b_free[0])
        s2b_ap = ap3(y2pre, 256, [[1024, 128], [0, 2], [1, 256]])
        for grp in range(16):
            q0 = grp * 2
            V(lambda e, q0=q0: e.tensor_tensor(
                rscr[:].rearrange("p (g c) -> p g c", g=2),
                convb[:].rearrange("p (g c) -> p g c", c=256)[:, q0:q0 + 2, :],
                s2b_ap, ALU.mult))
            def mk(q0=q0):
                def f(e):
                    e.tensor_scalar(
                        convb[:].rearrange("p (g c) -> p g c", c=256)[:, q0:q0 + 2, :],
                        rscr[:].rearrange("p (g c) -> p g c", g=2),
                        MAGIC, -MAGIC, ALU.add, ALU.add)
                return f
            V(mk())
        tok_h2q = seal_V(convb[:, 0:1])
        if dbg_win is not None and o == dbg_win:
            wt(SY, tok_h2q)
            SY(lambda e: e.dma_start(out=dbg_d["h2q"], in_=convb[:]).then_inc(s_dbg.h, 16))
            SY(lambda e: e.dma_start(out=dbg_d["bcols"], in_=bcols[:]).then_inc(s_dbg.h, 16))
            SY(lambda e: e.dma_start(out=dbg_d["s2b"], in_=y2pre[:, 256:512])
               .then_inc(s_dbg.h, 16))
            s_dbg.inc(48)
            wt(T, s_dbg.tok)
        for i in range(2):
            wt(T, tok_h2q)
            wt(T, pscd_free[0])
            for cc in range(NCC):
                T(lambda e, cc=cc, i=i: e.matmul(
                    PSC[:, 0:512], convb[:, cc * 256 + i * 128: cc * 256 + (i + 1) * 128],
                    w2q_sb[:, cc * 1024: cc * 1024 + 512],
                    start=(cc == 0), stop=(cc == NCC - 1)))
            for cc in range(NCC):
                def mk(cc=cc, i=i):
                    def f(e):
                        m = e.matmul(
                            PSD[:, 0:512],
                            convb[:, cc * 256 + i * 128: cc * 256 + (i + 1) * 128],
                            w2q_sb[:, cc * 1024 + 512: cc * 1024 + 1024],
                            start=(cc == 0), stop=(cc == NCC - 1))
                        if cc == NCC - 1:
                            m.then_inc(s_T.h, 1)
                    return f
                T(mk())
            tok_w2 = s_T.inc()
            w2_tok[o] = tok_w2
            tau = 2 * o + 1 + i
            slot = (tau // 2) % 2
            off = (tau % 2) * 1024
            wt(V, tok_w2)
            wt(V, y1r_tok[slot])
            if out_tok[0] is not None:
                wt(V, out_tok[0])
            pre_V(PSC[:, 0:1])
            pre_V(PSD[:, 0:1])
            V(lambda e, i=i, slot=slot, off=off: e.scalar_tensor_tensor(
                y2pre[:, 0:512], PSC[:, 0:512], BC(8 + i),
                y1r[slot][:, off:off + 512], ALU.mult, ALU.add))
            V(lambda e, i=i, slot=slot, off=off: e.scalar_tensor_tensor(
                y2pre[:, 512:1024], PSD[:, 0:512], BC(8 + i),
                y1r[slot][:, off + 512:off + 1024], ALU.mult, ALU.add))
            V(lambda e: e.tensor_reduce(BC(12), y2pre[:], AX.X, ALU.add))
            pscd_free[0] = tok_pre = seal_V(BC(12), 2)
            if dbg_win is not None and o == dbg_win and i == 0:
                wt(SY, tok_pre)
                SY(lambda e: e.dma_start(out=dbg_d["y2a"], in_=y2pre[:]).then_inc(s_dbg.h, 16))
                s_dbg.inc(16)
                wt(SC, s_dbg.tok)
            wt(SC, tok_pre)
            SC(lambda e, o=o: e.activation(y1r[o % 2][:, 0:1024], y2pre[:], AF.Square,
                                           accum_out=BC(13)))
            SC(lambda e: e.activation(BC(13), BC(13), AF.Identity, bias=0.0, scale=1.0 / 1024.0))
            SC(lambda e: e.activation(BC(14), BC(12), AF.Square, bias=0.0, scale=1.0 / 1024.0))
            tok_msq = seal_SC(BC(14))
            wt(V, tok_msq)
            V(lambda e: e.tensor_sub(BC(14), BC(13), BC(14)))
            tok_var = seal_V(BC(14))
            wt(SC, tok_var)
            SC(lambda e: e.activation(BC(14), BC(14), AF.Ln, bias=1e-5, scale=1.0))
            SC(lambda e: e.activation(BC(14), BC(14), AF.Exp, bias=0.0, scale=-0.5))
            SC(lambda e: e.activation(BC(15), BC(12), AF.Identity, bias=0.0, scale=-1.0 / 1024.0))
            SC(lambda e: e.activation(BC(15), BC(15), AF.Identity, bias=0.0, scale=BC(14)))
            SC(lambda e: e.activation(y2pre[:], y2pre[:], AF.Identity, bias=BC(15), scale=BC(14)))
            tok_out = seal_SC(y2pre[:, 0:1])
            wt(SY, tok_out)
            row0 = o * 256 + i * 128
            SY(lambda e, row0=row0: e.dma_start(out=out_d[row0:row0 + 128, :], in_=y2pre[:])
               .then_inc(s_out.h, 16))
            out_tok[0] = s_out.inc(16)
        y1r_free[o % 2] = s_V.tok
        y1r_free_sc[o % 2] = s_S.tok

    for w in range(N_WIN):
        if w >= 2:
            emit_out_tail(w - 2)
        emit_hwindow(w)
        if w >= 1:
            emit_out_head(w - 1)
    emit_out_tail(N_WIN - 2)

    SY(lambda e: e.wait_ge(s_out.h, s_out.n))
    b_stack.close()

    with nc.Block() as block:
        @block.tensor
        def _(eng):
            for f in st_T:
                f(eng)

        @block.vector
        def _(eng):
            for f in st_V:
                f(eng)

        @block.scalar
        def _(eng):
            for f in st_S:
                f(eng)

        @block.sync
        def _(eng):
            for f in st_Y:
                f(eng)

        @block.gpsimd
        def _(eng):
            for f in st_G:
                f(eng)

    g_stack.close()
    return nc


def _consts():
    j = np.arange(128)
    ident = np.eye(128, dtype=bf16)
    identf = np.eye(128, dtype=np.float32)
    mask1 = (j[:, None] <= j[None, :]).astype(bf16)
    mask4 = np.tile(mask1, (1, 4))
    tri = np.where(j[:, None] <= j[None, :], np.float32(-1.0 / 16.0), np.float32(0.0))
    onesf = np.full((128, 128), -1.0 / 16.0, dtype=np.float32)
    onesr = np.ones((1, 128), dtype=np.float32)
    onesc = np.ones((128, 1), dtype=np.float32)
    kcol = np.tile(np.asarray(KCOL_VALS, dtype=np.float32)[None, :], (128, 1))
    return dict(ident=ident, identf=identf, mask4=np.ascontiguousarray(mask4),
                tri=np.ascontiguousarray(tri.astype(np.float32)), onesf=onesf,
                onesr=onesr, onesc=onesc, kcol=np.ascontiguousarray(kcol))


def make_inmaps(L0_attn_x, Wq, Wk, Wv, Wg, Wgt, Wo, w1, conv_w, conv_b, w2):
    consts = _consts()
    w1t = np.ascontiguousarray(np.asarray(w1).T.astype(np.float32))
    w2t = np.ascontiguousarray(np.asarray(w2).T.astype(np.float32))
    cw = np.ascontiguousarray(
        np.asarray(conv_w)[:, 0, :].reshape(32, 128, 3).transpose(1, 0, 2).reshape(128, 96)
        .astype(np.float32))
    cb = np.ascontiguousarray(np.asarray(conv_b).reshape(32, 128).T.astype(np.float32))
    x = np.asarray(L0_attn_x)
    z128 = np.zeros((128, D), dtype=np.float32)
    zpre = np.zeros((N_PRE * 128, D), dtype=np.float32)
    in_maps = []
    for core in range(8):
        b, half = divmod(core, 2)
        if half == 0:
            xa = np.concatenate([z128, x[b, 0:4224]], axis=0)
            xp = zpre
        else:
            xa = np.concatenate([x[b, 3968:8192], z128], axis=0)
            xp = np.ascontiguousarray(x[b, 0:3968])
        m = dict(x_attn=np.ascontiguousarray(xa.astype(np.float32)), x_pre=xp,
                 wq=np.ascontiguousarray(np.asarray(Wq, dtype=np.float32)),
                 wk=np.ascontiguousarray(np.asarray(Wk, dtype=np.float32)),
                 wv=np.ascontiguousarray(np.asarray(Wv, dtype=np.float32)),
                 wg=np.ascontiguousarray(np.asarray(Wg, dtype=np.float32)),
                 wgt=np.ascontiguousarray(np.asarray(Wgt, dtype=np.float32)),
                 wo=np.ascontiguousarray(np.asarray(Wo, dtype=np.float32)),
                 w1t=w1t, w2t=w2t, cw=cw, cb=cb, **consts)
        in_maps.append(m)
    return in_maps


def kernel(L0_attn_x, L0_attn_Wq, L0_attn_Wk, L0_attn_Wv, L0_attn_Wg, L0_attn_Wgt,
           L0_attn_Wo, L0_ffn_w1, L0_ffn_conv_w, L0_ffn_conv_b, L0_ffn_w2):
    in_maps = make_inmaps(L0_attn_x, L0_attn_Wq, L0_attn_Wk, L0_attn_Wv, L0_attn_Wg,
                          L0_attn_Wgt, L0_attn_Wo, L0_ffn_w1, L0_ffn_conv_w,
                          L0_ffn_conv_b, L0_ffn_w2)
    nc = build_nc()
    res = run_bass_kernel_spmd(nc, in_maps, core_ids=list(range(8)))
    out = np.empty((B, S, D), dtype=np.float32)
    for core in range(8):
        b, half = divmod(core, 2)
        out[b, half * 4096:(half + 1) * 4096] = np.asarray(res.results[core]["out"],
                                                           dtype=np.float32)
    return out


# version 08 (final)
# speedup vs baseline: 1.0063x; 1.0063x over previous
import numpy as np
import ml_dtypes
import concourse.bass as bass
import concourse.mybir as mybir
from concourse.bass_utils import run_bass_kernel_spmd

F32 = mybir.dt.float32
BF16 = mybir.dt.bfloat16
AF = mybir.ActivationFunctionType
ALU = mybir.AluOpType
AX = mybir.AxisListType

B, S, D = 4, 8192, 1024
H = 16
CK = 128
N_PRE = 31
N_ATTN = 34
N_WIN = 17
DH = 4096
NCC = 32
MAGIC = float(np.float32(1.5 * 2.0 ** 23))
LN8INV = float(np.log(0.125))
KCOL_VALS = [1.0, -1.0, -0.5, 1.0 / 1024.0, 1.0 / 64.0, 1e-6, 1e-5, LN8INV, MAGIC,
             -1e-5, 4.8441870864585871, -1.0 / 1024.0]

bf16 = ml_dtypes.bfloat16


class Sem:
    __slots__ = ("h", "name", "n")

    def __init__(self, handle, name):
        self.h = handle
        self.name = name
        self.n = 0

    def inc(self, k=1):
        self.n += k
        return (self, self.n)

    @property
    def tok(self):
        return (self, self.n)


ZERO_TOK = None


def build_nc(dbg_y1=False, dbg_chunk=None, serialize=False, dbg_win=None, dbg_wq=False, dbg_scan=False):
    nc = bass.Bass("TRN2", target_bir_lowering=False)

    x_attn = nc.dram_tensor("x_attn", [N_ATTN * CK, D], F32, kind="ExternalInput").ap()
    x_pre = nc.dram_tensor("x_pre", [N_PRE * CK, D], F32, kind="ExternalInput").ap()
    wq_d = nc.dram_tensor("wq", [D, D], F32, kind="ExternalInput").ap()
    wk_d = nc.dram_tensor("wk", [D, D], F32, kind="ExternalInput").ap()
    wv_d = nc.dram_tensor("wv", [D, D], F32, kind="ExternalInput").ap()
    wg_d = nc.dram_tensor("wg", [D, D], F32, kind="ExternalInput").ap()
    wgt_d = nc.dram_tensor("wgt", [D, H], F32, kind="ExternalInput").ap()
    wo_d = nc.dram_tensor("wo", [D, D], F32, kind="ExternalInput").ap()
    w1t_d = nc.dram_tensor("w1t", [D, DH], F32, kind="ExternalInput").ap()
    w2t_d = nc.dram_tensor("w2t", [DH, D], F32, kind="ExternalInput").ap()
    cw_d = nc.dram_tensor("cw", [128, 3 * NCC], F32, kind="ExternalInput").ap()
    cb_d = nc.dram_tensor("cb", [128, NCC], F32, kind="ExternalInput").ap()
    ident_d = nc.dram_tensor("ident", [128, 128], BF16, kind="ExternalInput").ap()
    identf_d = nc.dram_tensor("identf", [128, 128], F32, kind="ExternalInput").ap()
    mask4_d = nc.dram_tensor("mask4", [128, 512], BF16, kind="ExternalInput").ap()
    tri_d = nc.dram_tensor("tri", [128, 128], F32, kind="ExternalInput").ap()
    onesf_d = nc.dram_tensor("onesf", [128, 128], F32, kind="ExternalInput").ap()
    onesr_d = nc.dram_tensor("onesr", [1, 128], F32, kind="ExternalInput").ap()
    onesc_d = nc.dram_tensor("onesc", [128, 1], F32, kind="ExternalInput").ap()
    kcol_d = nc.dram_tensor("kcol", [128, len(KCOL_VALS)], F32, kind="ExternalInput").ap()

    out_d = nc.dram_tensor("out", [4096, D], F32, kind="ExternalOutput").ap()
    y1buf = nc.dram_tensor("y1buf", [N_ATTN * CK, D], F32,
                           kind="ExternalOutput" if dbg_y1 else "Internal").ap()
    wq_kind = "ExternalOutput" if (dbg_win is not None or dbg_wq) else "Internal"
    w1q_h = nc.dram_tensor("w1q_h", [D, DH], BF16, kind=wq_kind).ap()
    w2q_h = nc.dram_tensor("w2q_h", [DH, D], BF16, kind=wq_kind).ap()
    dbg_d = {}
    if dbg_scan:
        dbg_d["Sdump"] = nc.dram_tensor("Sdump", [65 * 128, 512], F32, kind="ExternalOutput").ap()
        dbg_d["Cdump"] = nc.dram_tensor("Cdump", [65 * 128, 16], F32, kind="ExternalOutput").ap()
        dbg_d["Edump"] = nc.dram_tensor("Edump", [65 * 128, 32], F32, kind="ExternalOutput").ap()
        dbg_d["Ldump"] = nc.dram_tensor("Ldump", [65 * 128, 32], F32, kind="ExternalOutput").ap()
        dbg_d["Tdump"] = nc.dram_tensor("Tdump", [128, 256], F32, kind="ExternalOutput").ap()
        dbg_d["Bdump"] = nc.dram_tensor("Bdump", [65 * 128, 16], F32, kind="ExternalOutput").ap()
    if dbg_win is not None:
        for nm, wdt, wd in [("ytfm", BF16, 2048), ("ext", BF16, NCC * 258), ("conv", BF16, NCC * 256),
                            ("h2q", BF16, NCC * 256), ("bcols", F32, 16),
                            ("y2a", F32, 1024), ("s2b", F32, 256), ("silu", BF16, NCC * 256)]:
            dbg_d[nm] = nc.dram_tensor("dbg_" + nm, [128, wd], wdt,
                                       kind="ExternalOutput").ap()
    if dbg_chunk is not None:
        for nm, wdt, wd in [("q_tm", BF16, 1024), ("k_tm", BF16, 1024), ("v_tm", BF16, 1024),
                            ("g_f", F32, 1024), ("lnw", F32, 16), ("epos", F32, 16),
                            ("eneg", F32, 16), ("eB", F32, 16), ("eBmb", F32, 16),
                            ("kdec", BF16, 1024), ("kd_tm", BF16, 1024), ("qfm", BF16, 1024),
                            ("kfm", BF16, 1024), ("A_sb", BF16, 2048), ("o_tm", F32, 1024),
                            ("ro", F32, 16), ("sg", F32, 1024), ("gated", BF16, 1024),
                            ("y1t", F32, 1024), ("cols", F32, 16), ("sO", F32, 16),
                            ("S_old", BF16, 512), ("S_new", F32, 512), ("sso", F32, 16)]:
            dbg_d[nm] = nc.dram_tensor("dbg_" + nm, [128, wd], wdt,
                                       kind="ExternalOutput").ap()

    st_T, st_V, st_S, st_Y, st_G = [], [], [], [], []

    def T(f): st_T.append(f)
    def V(f): st_V.append(f)
    def SC(f): st_S.append(f)
    def SY(f): st_Y.append(f)
    def G(f): st_G.append(f)

    def wt(stream, tok):
        if tok is None:
            return
        sem, val = tok
        assert val <= sem.n, f"wait on future: {sem.name} {val} > {sem.n}"
        if val > 0:
            stream(lambda e, s=sem, v=val: e.wait_ge(s.h, v))

    from contextlib import ExitStack
    g_stack = ExitStack()

    def mk_sem(name):
        return Sem(g_stack.enter_context(nc.semaphore(name)), name)

    s_x = mk_sem("s_x")
    s_xb = mk_sem("s_xb")
    s_w = mk_sem("s_w")
    s_sl = mk_sem("s_sl")
    s_qw = mk_sem("s_qw")
    s_y1 = mk_sem("s_y1")
    s_y1r = mk_sem("s_y1r")
    s_out = mk_sem("s_out")
    s_T = mk_sem("s_T")
    s_V = mk_sem("s_V")
    s_S = mk_sem("s_S")
    s_bar = mk_sem("s_bar")
    s_dbg = mk_sem("s_dbg")
    s_ser = mk_sem("s_ser")
    ser_n = [0]

    def ser_barrier():
        if not serialize:
            return
        for st_, eng in ((T, "t"), (V, "v"), (SC, "s")):
            if eng == "t":
                wt(T, psm_free[0])
                st_(lambda e: e.matmul(PSM[:, 101:102], onesr[:], onesc[0:1, 0:1],
                                       start=True, stop=True).then_inc(s_ser.h, 1))
            elif eng == "v":
                st_(lambda e: e.engine_nop().then_inc(s_ser.h, 1))
            else:
                st_(lambda e: e.memzero(u16[0:1, 0:1]).then_inc(s_ser.h, 1))
        s_ser.inc(3)
        ser_n[0] += 3
        for st_ in (T, V, SC):
            wt(st_, (s_ser, ser_n[0]))

    PSA = g_stack.enter_context(nc.psum_tensor("PSA", [128, 1024], F32))
    PSB = g_stack.enter_context(nc.psum_tensor("PSB", [128, 1024], F32))
    PSC = g_stack.enter_context(nc.psum_tensor("PSC", [128, 512], F32))
    PSD = g_stack.enter_context(nc.psum_tensor("PSD", [128, 512], F32))
    PTR = g_stack.enter_context(nc.psum_tensor("PTR", [128, 1024], BF16))
    PSM = g_stack.enter_context(nc.psum_tensor("PSM", [128, 512], F32))

    ident = g_stack.enter_context(nc.sbuf_tensor("ident_s", [128, 128], BF16))
    identf = g_stack.enter_context(nc.sbuf_tensor("identf_s", [128, 128], F32))
    onesr = g_stack.enter_context(nc.sbuf_tensor("onesr_s", [1, 128], F32))
    onesc = g_stack.enter_context(nc.sbuf_tensor("onesc_s", [128, 1], F32))
    cw_sb = g_stack.enter_context(nc.sbuf_tensor("cw_sb", [128, 3 * NCC], F32))
    cb_sb = g_stack.enter_context(nc.sbuf_tensor("cb_sb", [128, NCC], F32))
    sw1c = g_stack.enter_context(nc.sbuf_tensor("sw1c", [128, 1], F32))
    sw2c = g_stack.enter_context(nc.sbuf_tensor("sw2c", [128, 1], F32))
    a1fac = g_stack.enter_context(nc.sbuf_tensor("a1fac", [128, 1], F32))
    a2fac = g_stack.enter_context(nc.sbuf_tensor("a2fac", [128, 1], F32))
    kcol = g_stack.enter_context(nc.sbuf_tensor("kcol_s", [128, len(KCOL_VALS)], F32))
    for _j, _v in enumerate(KCOL_VALS):
        nc.const_aps.register(F32, _v, kcol[:, _j:_j + 1])
    vdead = g_stack.enter_context(nc.sbuf_tensor("vdead", [128, 2], F32))
    scdead = g_stack.enter_context(nc.sbuf_tensor("scdead", [128, 2], F32))

    a_stack = ExitStack()
    wq_sb = a_stack.enter_context(nc.sbuf_tensor("wq_sb", [128, 8 * 1024], BF16))
    wk_sb = a_stack.enter_context(nc.sbuf_tensor("wk_sb", [128, 8 * 1024], BF16))
    wv_sb = a_stack.enter_context(nc.sbuf_tensor("wv_sb", [128, 8 * 1024], BF16))
    wg_sb = a_stack.enter_context(nc.sbuf_tensor("wg_sb", [128, 8 * 1024], BF16))
    wo_sb = a_stack.enter_context(nc.sbuf_tensor("wo_sb", [128, 8 * 1024], BF16))
    wgt_sb = a_stack.enter_context(nc.sbuf_tensor("wgt_sb", [128, 8 * 16], BF16))
    mask4 = a_stack.enter_context(nc.sbuf_tensor("mask4_s", [128, 512], BF16))
    tri = a_stack.enter_context(nc.sbuf_tensor("tri_s", [128, 128], F32))
    onesf = a_stack.enter_context(nc.sbuf_tensor("onesf_s", [128, 128], F32))
    xf = [a_stack.enter_context(nc.sbuf_tensor(f"xf{i}", [128, 1024], F32)) for i in range(2)]
    xb = [a_stack.enter_context(nc.sbuf_tensor(f"xb{i}", [128, 1024], BF16)) for i in range(2)]
    xfm = a_stack.enter_context(nc.sbuf_tensor("xfm", [128, 1024], BF16))
    q_tm = a_stack.enter_context(nc.sbuf_tensor("q_tm", [128, 1024], BF16))
    k_tm = a_stack.enter_context(nc.sbuf_tensor("k_tm", [128, 1024], BF16))
    v_tm = a_stack.enter_context(nc.sbuf_tensor("v_tm", [128, 1024], BF16))
    g_f = a_stack.enter_context(nc.sbuf_tensor("g_f", [128, 1024], F32))
    kdec = a_stack.enter_context(nc.sbuf_tensor("kdec", [128, 1024], BF16))
    kd_tm = a_stack.enter_context(nc.sbuf_tensor("kd_tm", [128, 1024], BF16))
    qfm = a_stack.enter_context(nc.sbuf_tensor("qfm", [128, 1024], BF16))
    kfm = a_stack.enter_context(nc.sbuf_tensor("kfm", [128, 1024], BF16))
    A_sb = a_stack.enter_context(nc.sbuf_tensor("A_sb", [128, 2048], BF16))
    o_tm = a_stack.enter_context(nc.sbuf_tensor("o_tm", [128, 1024], F32))
    sq_scr = a_stack.enter_context(nc.sbuf_tensor("sq_scr", [128, 1024], F32))
    sg = a_stack.enter_context(nc.sbuf_tensor("sg", [128, 1024], F32))
    gated = a_stack.enter_context(nc.sbuf_tensor("gated", [128, 1024], BF16))
    gfm = a_stack.enter_context(nc.sbuf_tensor("gfm", [128, 1024], BF16))
    y1t = a_stack.enter_context(nc.sbuf_tensor("y1t", [128, 1024], F32))
    y1o = a_stack.enter_context(nc.sbuf_tensor("y1o", [128, 1024], F32))
    S_sb = [a_stack.enter_context(nc.sbuf_tensor(f"S{i}", [128, 512], F32)) for i in range(2)]
    S_bf = [a_stack.enter_context(nc.sbuf_tensor(f"Sbf{i}", [128, 512], BF16)) for i in range(2)]
    u16 = a_stack.enter_context(nc.sbuf_tensor("u16", [128, 16], F32))
    lnw = a_stack.enter_context(nc.sbuf_tensor("lnw", [128, 16], F32))
    epos = a_stack.enter_context(nc.sbuf_tensor("epos", [128, 16], F32))
    eneg = a_stack.enter_context(nc.sbuf_tensor("eneg", [128, 16], F32))
    eB = a_stack.enter_context(nc.sbuf_tensor("eB", [128, 16], F32))
    eBmb = a_stack.enter_context(nc.sbuf_tensor("eBmb", [128, 16], F32))
    eBS = a_stack.enter_context(nc.sbuf_tensor("eBS", [128, 8], F32))
    sO = a_stack.enter_context(nc.sbuf_tensor("sO", [128, 16], F32))
    sso = a_stack.enter_context(nc.sbuf_tensor("sso", [128, 16], F32))
    ro = a_stack.enter_context(nc.sbuf_tensor("ro", [128, 16], F32))
    cols = a_stack.enter_context(nc.sbuf_tensor("cols", [128, 16], F32))
    wsl = [a_stack.enter_context(nc.sbuf_tensor(f"wsl{i}", [128, 2048], F32)) for i in range(2)]
    wqs = [a_stack.enter_context(nc.sbuf_tensor(f"wqs{i}", [128, 2048], BF16)) for i in range(2)]
    absum = a_stack.enter_context(nc.sbuf_tensor("absum", [128, 32], F32))
    scal1 = a_stack.enter_context(nc.sbuf_tensor("scal1", [1, 8], F32))

    def C(i):
        return cols[:, i:i + 1]

    ptr_free = [None]
    psa_free = [None]
    psb_free = [None]
    psm_free = [None]
    psc_gate = [None]
    psd_gate = [None]
    sbf_tok = [None, None]
    wsl_free = [None, None]
    wqs_free = [None, None]
    xf_free = {}
    xtr_done = {}
    xb_tok = {}
    xf_tok = {}
    y1_total = [0]

    for dst, src in ((ident, ident_d), (identf, identf_d), (mask4, mask4_d),
                     (tri, tri_d), (onesf, onesf_d), (onesr, onesr_d),
                     (onesc, onesc_d), (cw_sb, cw_d), (cb_sb, cb_d), (kcol, kcol_d)):
        SY(lambda e, d=dst, s=src: e.dma_start(out=d[:], in_=s).then_inc(s_w.h, 16))
        s_w.inc(16)
    for dst, src, w_ in ((wq_sb, wq_d, 1024), (wk_sb, wk_d, 1024), (wv_sb, wv_d, 1024),
                         (wg_sb, wg_d, 1024), (wo_sb, wo_d, 1024), (wgt_sb, wgt_d, 16)):
        G(lambda e, d=dst, s=src, w=w_:
          e.dma_start(out=d[:].rearrange("p (k w) -> p k w", w=w),
                      in_=s.rearrange("(k p) w -> p k w", p=128)).then_inc(s_w.h, 16))
        s_w.inc(16)
    V(lambda e: e.memset(S_sb[0][:], 0.0))
    V(lambda e: e.memset(S_sb[1][:], 0.0))
    V(lambda e: e.memset(S_bf[0][:], 0.0))
    V(lambda e: e.memset(S_bf[1][:], 0.0).then_inc(s_V.h, 1))
    vS_init = s_V.inc()

    for t_ in (lnw, u16, eB, eBmb, eBS, epos, eneg, sO, sso, ro, cols, absum,
               kdec, kd_tm, A_sb, o_tm, gated, sg, sq_scr, q_tm, k_tm, v_tm,
               g_f, qfm, kfm, gfm, xfm, y1t, vdead, scdead):
        V(lambda e, t=t_: e.memset(t[:], 0.0))
    for _ in range(4):
        SC(lambda e: e.activation(y1o[:], y1o[:], AF.Exp, bias=0.0, scale=0.0))

    wt(T, s_w.tok)
    wt(V, s_w.tok)
    wt(SC, s_w.tok)

    for ps_, nt_ in ((PSA, 8), (PSB, 8), (PSC, 4), (PSD, 4), (PSM, 4)):
        for i_ in range(nt_):
            T(lambda e, ps=ps_, i=i_: e.matmul(ps[:, i * 128:(i + 1) * 128],
                                               identf[:], identf[:],
                                               start=True, stop=True))
    for i_ in range(8):
        T(lambda e, i=i_: e.transpose(PTR[:, i * 128:(i + 1) * 128],
                                      ident[:], ident[:]))

    def mm_proj(dst_ps, w_sb):
        for n in range(2):
            for i in range(8):
                last = (n == 1 and i == 7)
                def mk(n=n, i=i, d=dst_ps, w=w_sb, last=last):
                    def f(e):
                        m = e.matmul(d[:, n * 512:(n + 1) * 512],
                                     xfm[:, i * 128:(i + 1) * 128],
                                     w[:, i * 1024 + n * 512: i * 1024 + (n + 1) * 512],
                                     start=(i == 0), stop=(i == 7))
                        if last:
                            m.then_inc(s_T.h, 1)
                    return f
                T(mk())
        return s_T.inc()

    def mm_tr8(src_sb):
        for i in range(8):
            def mk(i=i, s=src_sb):
                def f(e):
                    m = e.transpose(PTR[:, i * 128:(i + 1) * 128],
                                    s[:, i * 128:(i + 1) * 128], ident[:])
                    if i == 7:
                        m.then_inc(s_T.h, 1)
                return f
            T(mk())
        return s_T.inc()

    def head_fm(hsb, h):
        return hsb[(h % 2) * 64:(h % 2) * 64 + 64, (h // 2) * 128:(h // 2) * 128 + 128]

    def ap3(t_sb, off, dims):
        a = t_sb[:]
        return bass.AP(a.tensor, a.offset + off, dims)

    def pe_guard(dst_ps, col0):
        T(lambda e: e.matmul(dst_ps[:, col0:col0 + 128], ident[:], ident[:],
                             start=True, stop=True))
        T(lambda e: e.matmul(dst_ps[:, col0 + 128:col0 + 256], ident[:], ident[:],
                             start=True, stop=True).then_inc(s_T.h, 1))
        return s_T.inc()

    def seal_V(col, n=1, p=128):
        for _ in range(n - 1):
            V(lambda e, c=col, p=p: e.tensor_copy(vdead[0:p, 0:1], c))
        V(lambda e, c=col, p=p: e.tensor_copy(vdead[0:p, 0:1], c).then_inc(s_V.h, 1))
        return s_V.inc()

    def seal_SC(col, p=128):
        SC(lambda e, c=col, p=p: e.copy(scdead[0:p, 0:1], c).then_inc(s_S.h, 1))
        return s_S.inc()

    def pre_V(col, p=128):
        V(lambda e, c=col, p=p: e.tensor_copy(vdead[0:p, 1:2], c))

    def pre_SC(col, p=128):
        SC(lambda e, c=col, p=p: e.copy(scdead[0:p, 1:2], c))

    def emit_gt_bc(tok_gt_mm, with_eBS):
        wt(SC, tok_gt_mm)
        SC(lambda e: e.activation(u16[:], PSD[:, 0:16], AF.Exp, bias=0.0, scale=-1.0))
        SC(lambda e: e.activation(lnw[:], u16[:], AF.Ln, bias=1.0, scale=1.0))
        tok_lnw = seal_SC(lnw[:, 0:1])
        wt(T, tok_lnw)
        T(lambda e: e.matmul(PSD[:, 16:32], tri[:], lnw[:], start=True, stop=True))
        T(lambda e: e.matmul(PSD[:, 32:48], onesf[:], lnw[:], start=True, stop=True))
        T(lambda e: e.matmul(PSD[:, 48:56], onesf[:],
                             ap3(lnw, 0, [[16, 128], [2, 8]]), start=True, stop=True))
        T(lambda e: e.matmul(PSD[:, 56:64], onesf[:],
                             ap3(lnw, 1, [[16, 128], [2, 8]]), start=True, stop=True)
          .then_inc(s_T.h, 1))
        s_T.inc()
        tok_bc = pe_guard(PSD, 100)
        wt(SC, tok_bc)
        SC(lambda e: e.activation(epos[:], PSD[:, 16:32], AF.Exp, bias=0.0, scale=1.0))
        SC(lambda e: e.activation(eneg[:], PSD[:, 16:32], AF.Exp, bias=0.0, scale=-1.0))
        SC(lambda e: e.activation(eB[:], PSD[:, 32:48], AF.Exp, bias=0.0, scale=1.0))
        SC(lambda e: e.activation(eBS[0:64, :], PSD[0:64, 48:56], AF.Exp, bias=0.0, scale=1.0))
        SC(lambda e: e.activation(eBS[64:128, :], PSD[64:128, 56:64], AF.Exp,
                                  bias=0.0, scale=1.0))
        tok_exps = seal_SC(eBS[:, 0:1])
        return tok_exps

    def emit_rsqrt(dst, src, n, eps, bias2=0.0, inc=False):
        SC(lambda e, d=dst, s=src: e.activation(d, s, AF.Ln, bias=eps, scale=1.0 / n))
        def f(e, d=dst):
            m = e.activation(d, d, AF.Exp, bias=bias2, scale=-0.5)
            if inc:
                m.then_inc(s_S.h, 1)
        SC(f)
        return s_S.inc() if inc else None

    def emit_kv_S(scan_idx, tok_kd, tok_exps):
        wt(T, psm_free[0])
        wt(T, tok_kd)
        for h in range(H):
            def mk(h=h):
                def f(e):
                    m = e.matmul(
                        PSM[(h % 2) * 64:(h % 2) * 64 + 64, (h // 2) * 64:(h // 2) * 64 + 64],
                        kd_tm[:, h * 64:(h + 1) * 64], v_tm[:, h * 64:(h + 1) * 64],
                        start=True, stop=True)
                    if h == H - 1:
                        m.then_inc(s_T.h, 1)
                return f
            T(mk())
        s_T.inc()
        tok_kv = pe_guard(PSA, 0)
        sold, snew = S_sb[scan_idx % 2], S_sb[(scan_idx + 1) % 2]
        wt(V, tok_kv)
        wt(V, tok_exps)
        if dbg_scan:
            V(lambda e: e.tensor_copy(sso[:], PSD[:, 32:48]).then_inc(s_V.h, 1))
            tok_dbgc = s_V.inc()
            wt(SY, tok_dbgc)
            SY(lambda e, si=scan_idx: e.dma_start(
                out=dbg_d["Bdump"][si * 128:(si + 1) * 128, :], in_=sso[:]).then_inc(s_dbg.h, 16))
            s_dbg.inc(16)
        V(lambda e, so=sold, sn=snew: e.tensor_tensor(
            sn[:], so[:], ap3(eBS, 0, [[8, 128], [1, 8], [0, 64]]), ALU.mult))
        V(lambda e, sn=snew: e.tensor_add(sn[:], sn[:], PSM[:, 0:512]))
        V(lambda e, sn=snew, sb=S_bf[(scan_idx + 1) % 2]: e.tensor_copy(sb[:], sn[:]))
        psm_free[0] = sbf_tok[(scan_idx + 1) % 2] = seal_V(S_bf[(scan_idx + 1) % 2][:, 0:1])
        if dbg_scan:
            wt(SY, psm_free[0])
            SY(lambda e, sn=snew, si=scan_idx: e.dma_start(
                out=dbg_d["Sdump"][si * 128:(si + 1) * 128, :], in_=sn[:]).then_inc(s_dbg.h, 16))
            SY(lambda e, si=scan_idx: e.dma_start(
                out=dbg_d["Cdump"][si * 128:(si + 1) * 128, :], in_=cols[:]).then_inc(s_dbg.h, 16))
            SY(lambda e, si=scan_idx: e.dma_start(
                out=dbg_d["Edump"][si * 128:(si + 1) * 128, 0:8], in_=eBS[:]).then_inc(s_dbg.h, 16))
            SY(lambda e, si=scan_idx: e.dma_start(
                out=dbg_d["Edump"][si * 128:(si + 1) * 128, 8:24], in_=eB[:]).then_inc(s_dbg.h, 16))
            SY(lambda e, si=scan_idx: e.dma_start(
                out=dbg_d["Ldump"][si * 128:(si + 1) * 128, 0:16], in_=lnw[:]).then_inc(s_dbg.h, 16))
            SY(lambda e, si=scan_idx: e.dma_start(
                out=dbg_d["Ldump"][si * 128:(si + 1) * 128, 16:32], in_=u16[:]).then_inc(s_dbg.h, 16))
            s_dbg.inc(32)
            if scan_idx == 0:
                SY(lambda e: e.dma_start(out=dbg_d["Tdump"][:, 0:128], in_=tri[:]).then_inc(s_dbg.h, 16))
                SY(lambda e: e.dma_start(out=dbg_d["Tdump"][:, 128:256], in_=onesf[:]).then_inc(s_dbg.h, 16))
                s_dbg.inc(32)
            s_dbg.inc(64)
            wt(V, s_dbg.tok)
            wt(SC, s_dbg.tok)
        return psm_free[0]

    def slab_src(i):
        if i < 16:
            t_, hh = divmod(i, 2)
            return w1t_d[t_ * 128:(t_ + 1) * 128, hh * 2048:(hh + 1) * 2048]
        j = i - 16
        return bass.AP(w2t_d.tensor, w2t_d.offset + j * 256 * 1024,
                       [[1024, 128], [128 * 1024, 2], [1, 1024]])

    def slab_dst(i):
        if i < 16:
            t_, hh = divmod(i, 2)
            return w1q_h[t_ * 128:(t_ + 1) * 128, hh * 2048:(hh + 1) * 2048]
        j = i - 16
        return bass.AP(w2q_h.tensor, w2q_h.offset + j * 256 * 1024,
                       [[1024, 128], [128 * 1024, 2], [1, 1024]])

    def emit_q1_slab(i):
        sl = i % 2
        wt(SY, wsl_free[sl])
        SY(lambda e, i=i, sl=sl: e.dma_start(out=wsl[sl][:], in_=slab_src(i))
           .then_inc(s_sl.h, 16))
        tok_ld = s_sl.inc(16)
        wt(V, tok_ld)
        V(lambda e, i=i, sl=sl: e.tensor_reduce(absum[:, i:i + 1], wsl[sl][:], AX.X, ALU.add,
                                                apply_absolute_value=True))
        wsl_free[sl] = seal_V(absum[:, i:i + 1], 2)

    def emit_mean_math():
        wt(T, wsl_free[0])
        wt(T, wsl_free[1])
        wt(T, psm_free[0])
        T(lambda e: e.matmul(PSM[0:1, 64:96], onesc[:], absum[:], start=True, stop=True)
          .then_inc(s_T.h, 1))
        s_T.inc()
        tok_sum = pe_guard(PSM, 128)
        wt(V, tok_sum)
        V(lambda e: e.tensor_reduce(scal1[:, 0:1], PSM[0:1, 64:80], AX.X, ALU.add))
        V(lambda e: e.tensor_reduce(scal1[:, 1:2], PSM[0:1, 80:96], AX.X, ALU.add))
        wt(V, seal_V(scal1[:, 1:2], 2, p=1))
        V(lambda e: e.tensor_scalar(scal1[:, 2:3], scal1[:, 0:1], 1.0 / (DH * D), 1e-5,
                                    ALU.mult, ALU.max))
        V(lambda e: e.tensor_scalar(scal1[:, 3:4], scal1[:, 1:2], 1.0 / (DH * D), 1e-5,
                                    ALU.mult, ALU.max))
        V(lambda e: e.reciprocal(scal1[:, 4:5], scal1[:, 2:3]))
        V(lambda e: e.reciprocal(scal1[:, 5:6], scal1[:, 3:4]))
        wt(V, seal_V(scal1[:, 5:6], 2, p=1))
        V(lambda e: e.tensor_scalar(scal1[:, 6:7], scal1[:, 2:3], 1.0 / 127.0, None, ALU.mult))
        V(lambda e: e.tensor_scalar(scal1[:, 7:8], scal1[:, 3:4], 1.0 / 127.0, None, ALU.mult))
        tok_sc = seal_V(scal1[:, 7:8], p=1)
        wt(T, tok_sc)
        T(lambda e: e.matmul(PSM[:, 96:97], onesr[:], scal1[:, 4:5], start=True, stop=True))
        T(lambda e: e.matmul(PSM[:, 97:98], onesr[:], scal1[:, 5:6], start=True, stop=True))
        T(lambda e: e.matmul(PSM[:, 98:99], onesr[:], scal1[:, 6:7], start=True, stop=True))
        T(lambda e: e.matmul(PSM[:, 99:100], onesr[:], scal1[:, 7:8], start=True, stop=True)
          .then_inc(s_T.h, 1))
        s_T.inc()
        tok_bc2 = pe_guard(PSM, 128)
        wt(V, tok_bc2)
        V(lambda e: e.tensor_copy(sw1c[:], PSM[:, 96:97]))
        V(lambda e: e.tensor_copy(sw2c[:], PSM[:, 97:98]))
        V(lambda e: e.tensor_copy(a1fac[:], PSM[:, 98:99]))
        V(lambda e: e.tensor_copy(a2fac[:], PSM[:, 99:100]))
        psm_free[0] = seal_V(a2fac[:, 0:1])
        return psm_free[0]

    def emit_q2_slab(i, tok_scales):
        sl = i % 2
        wt(SY, wsl_free[sl])
        SY(lambda e, i=i, sl=sl: e.dma_start(out=wsl[sl][:], in_=slab_src(i))
           .then_inc(s_sl.h, 16))
        tok_ld = s_sl.inc(16)
        swc = sw1c if i < 16 else sw2c
        wt(V, tok_ld)
        wt(V, tok_scales)
        wt(V, wqs_free[sl])
        V(lambda e, sl=sl, w=swc: e.tensor_scalar(wsl[sl][:], wsl[sl][:], w[:, 0:1], MAGIC,
                                                  ALU.mult, ALU.add))
        V(lambda e, sl=sl: e.tensor_scalar(wsl[sl][:], wsl[sl][:], -MAGIC, 1.0,
                                           ALU.add, ALU.min))
        V(lambda e, sl=sl: e.tensor_scalar(wqs[sl][:], wsl[sl][:], -1.0, None, ALU.max))
        wsl_free[sl] = seal_V(wqs[sl][:, 0:1])
        wt(SY, wsl_free[sl])
        SY(lambda e, i=i, sl=sl: e.dma_start(out=slab_dst(i), in_=wqs[sl][:])
           .then_inc(s_qw.h, 16))
        wqs_free[sl] = s_qw.inc(16)

    q_state = {"q1": 0, "mean": False, "q2": 0, "scales": None}

    def interleave_quant():
        if q_state["q1"] < 32:
            emit_q1_slab(q_state["q1"])
            emit_q1_slab(q_state["q1"] + 1)
            q_state["q1"] += 2
        elif not q_state["mean"]:
            q_state["scales"] = emit_mean_math()
            q_state["mean"] = True
        elif q_state["q2"] < 32:
            emit_q2_slab(q_state["q2"], q_state["scales"])
            emit_q2_slab(q_state["q2"] + 1, q_state["scales"])
            q_state["q2"] += 2

    def emit_prefix_chunk(p, scan_idx):
        sl = p % 2
        wt(T, xb_tok[("pre", p)])
        wt(T, ptr_free[0])
        tok_xtr = mm_tr8(xb[sl])
        xtr_done[("pre", p)] = tok_xtr
        wt(V, tok_xtr)
        pre_V(PTR[:, 0:1])
        V(lambda e: e.tensor_copy(xfm[:], PTR[:]))
        ptr_free[0] = tok_xfm = seal_V(xfm[:, 0:1])
        wt(T, tok_xfm)
        wt(T, psa_free[0])
        tok_k = mm_proj(PSA, wk_sb)
        wt(T, psb_free[0])
        tok_v = mm_proj(PSB, wv_sb)
        wt(T, psd_gate[0])
        for i in range(8):
            def mk(i=i):
                def f(e):
                    m = e.matmul(PSD[:, 0:16], xfm[:, i * 128:(i + 1) * 128],
                                 wgt_sb[:, i * 16:(i + 1) * 16],
                                 start=(i == 0), stop=(i == 7))
                    if i == 7:
                        m.then_inc(s_T.h, 1)
                return f
            T(mk())
        s_T.inc()
        tok_gt = pe_guard(PSD, 100)
        wt(SC, tok_k)
        pre_SC(PSA[:, 0:1])
        SC(lambda e: e.copy(k_tm[:], PSA[:]))
        psa_free[0] = seal_SC(k_tm[:, 0:1])
        SC(lambda e: e.activation(sq_scr[:], k_tm[:], AF.Square, accum_out=C(1)))
        wt(V, tok_v)
        pre_V(PSB[:, 0:1])
        V(lambda e: e.tensor_copy(v_tm[:], PSB[:]))
        psb_free[0] = tok_vtm = seal_V(v_tm[:, 0:1])
        wt(SC, tok_vtm)
        SC(lambda e: e.activation(sq_scr[:], v_tm[:], AF.Square, accum_out=C(2)))
        tok_exps = emit_gt_bc(tok_gt, True)
        psd_gate[0] = tok_exps
        emit_rsqrt(C(4), C(1), 1024.0, 1e-6)
        emit_rsqrt(C(5), C(2), 1024.0, 1e-6)
        SC(lambda e: e.activation(C(6), C(4), AF.Identity, bias=0.0, scale=C(5)))
        tok_rkv = seal_SC(C(6))
        wt(V, tok_rkv)
        V(lambda e: e.scalar_tensor_tensor(
            kdec[:], k_tm[:], C(6), ap3(eneg, 0, [[16, 128], [1, 16], [0, 64]]),
            ALU.mult, ALU.mult))
        V(lambda e: e.tensor_tensor(
            kd_tm[:], kdec[:], ap3(eB, 0, [[16, 128], [1, 16], [0, 64]]), ALU.mult))
        tok_kd = seal_V(kd_tm[:, 0:1])
        emit_kv_S(scan_idx, tok_kd, tok_exps)

    def emit_attn_chunk(c, scan_idx):
        sl = c % 2
        wt(T, xb_tok[("attn", c)])
        wt(T, ptr_free[0])
        tok_xtr = mm_tr8(xb[sl])
        xtr_done[("attn", c)] = tok_xtr
        wt(V, tok_xtr)
        pre_V(PTR[:, 0:1])
        V(lambda e: e.tensor_copy(xfm[:], PTR[:]))
        ptr_free[0] = tok_xfm = seal_V(xfm[:, 0:1])
        wt(T, tok_xfm)
        wt(T, psa_free[0])
        tok_q = mm_proj(PSA, wq_sb)
        wt(T, psb_free[0])
        tok_k = mm_proj(PSB, wk_sb)
        wt(SC, tok_q)
        pre_SC(PSA[:, 0:1])
        SC(lambda e: e.copy(q_tm[:], PSA[:]))
        tok_qtm = seal_SC(q_tm[:, 0:1])
        SC(lambda e: e.activation(sq_scr[:], q_tm[:], AF.Square, accum_out=C(0)))
        wt(V, tok_k)
        pre_V(PSB[:, 0:1])
        V(lambda e: e.tensor_copy(k_tm[:], PSB[:]))
        tok_ktm = seal_V(k_tm[:, 0:1])
        wt(SC, tok_ktm)
        SC(lambda e: e.activation(sq_scr[:], k_tm[:], AF.Square, accum_out=C(1))
           .then_inc(s_S.h, 1))
        s_S.inc()
        wt(T, psd_gate[0])
        for i in range(8):
            def mk(i=i):
                def f(e):
                    m = e.matmul(PSD[:, 0:16], xfm[:, i * 128:(i + 1) * 128],
                                 wgt_sb[:, i * 16:(i + 1) * 16],
                                 start=(i == 0), stop=(i == 7))
                    if i == 7:
                        m.then_inc(s_T.h, 1)
                return f
            T(mk())
        s_T.inc()
        tok_gt = pe_guard(PSD, 100)
        wt(SC, tok_gt)
        SC(lambda e: e.activation(u16[:], PSD[:, 0:16], AF.Exp, bias=0.0, scale=-1.0))
        SC(lambda e: e.activation(lnw[:], u16[:], AF.Ln, bias=1.0, scale=1.0))
        tok_lnw = seal_SC(lnw[:, 0:1])
        wt(T, tok_qtm)
        tok_v = mm_proj(PSA, wv_sb)
        wt(T, tok_ktm)
        tok_g = mm_proj(PSB, wg_sb)
        wt(T, tok_lnw)
        T(lambda e: e.matmul(PSD[:, 16:32], tri[:], lnw[:], start=True, stop=True))
        T(lambda e: e.matmul(PSD[:, 32:48], onesf[:], lnw[:], start=True, stop=True))
        T(lambda e: e.matmul(PSD[:, 48:56], onesf[:],
                             ap3(lnw, 0, [[16, 128], [2, 8]]), start=True, stop=True))
        T(lambda e: e.matmul(PSD[:, 56:64], onesf[:],
                             ap3(lnw, 1, [[16, 128], [2, 8]]), start=True, stop=True)
          .then_inc(s_T.h, 1))
        s_T.inc()
        tok_bc = pe_guard(PSD, 100)
        wt(SC, tok_bc)
        SC(lambda e: e.activation(epos[:], PSD[:, 16:32], AF.Exp, bias=0.0, scale=1.0))
        SC(lambda e: e.activation(eneg[:], PSD[:, 16:32], AF.Exp, bias=0.0, scale=-1.0))
        SC(lambda e: e.activation(eB[:], PSD[:, 32:48], AF.Exp, bias=0.0, scale=1.0))
        SC(lambda e: e.activation(eBS[0:64, :], PSD[0:64, 48:56], AF.Exp, bias=0.0, scale=1.0))
        SC(lambda e: e.activation(eBS[64:128, :], PSD[64:128, 56:64], AF.Exp,
                                  bias=0.0, scale=1.0))
        tok_exps = seal_SC(eBS[:, 0:1])
        wt(V, tok_v)
        pre_V(PSA[:, 0:1])
        V(lambda e: e.tensor_copy(v_tm[:], PSA[:]))
        psa_free[0] = tok_vtm = seal_V(v_tm[:, 0:1])
        wt(V, tok_exps)
        V(lambda e: e.tensor_tensor(
            kdec[:], k_tm[:], ap3(eneg, 0, [[16, 128], [1, 16], [0, 64]]), ALU.mult))
        tok_kdec = seal_V(kdec[:, 0:1])
        wt(SC, tok_vtm)
        SC(lambda e: e.activation(sq_scr[:], v_tm[:], AF.Square, accum_out=C(2)))
        wt(SC, tok_g)
        pre_SC(PSB[:, 0:1])
        SC(lambda e: e.copy(g_f[:], PSB[:]))
        psb_free[0] = seal_SC(g_f[:, 0:1])
        emit_rsqrt(C(3), C(0), 1024.0, 1e-6, bias2=LN8INV)
        emit_rsqrt(C(4), C(1), 1024.0, 1e-6)
        emit_rsqrt(C(5), C(2), 1024.0, 1e-6)
        SC(lambda e: e.activation(C(6), C(4), AF.Identity, bias=0.0, scale=C(5)))
        tok_rkv = seal_SC(C(6))
        wt(V, tok_rkv)
        V(lambda e: e.scalar_tensor_tensor(
            kd_tm[:], kdec[:], C(6), ap3(eB, 0, [[16, 128], [1, 16], [0, 64]]),
            ALU.mult, ALU.mult))
        V(lambda e: e.tensor_scalar(sO[:], epos[:], C(3), None, ALU.mult))
        tok_kd = seal_V(kd_tm[:, 0:1])
        wt(T, ptr_free[0])
        wt(T, tok_qtm)
        tok_qtr = mm_tr8(q_tm)
        wt(V, tok_qtr)
        pre_V(PTR[:, 0:1])
        V(lambda e: e.tensor_copy(qfm[:], PTR[:]))
        ptr_free[0] = seal_V(qfm[:, 0:1])
        wt(T, ptr_free[0])
        wt(T, tok_kdec)
        tok_ktr = mm_tr8(kdec)
        wt(V, tok_ktr)
        pre_V(PTR[:, 0:1])
        V(lambda e: e.tensor_copy(kfm[:], PTR[:]))
        ptr_free[0] = tok_kfm = seal_V(kfm[:, 0:1])
        emit_kv_S(scan_idx, tok_kd, tok_exps)
        wt(T, tok_kfm)
        wt(T, psc_gate[0])
        wt(T, tok_exps)
        evac_toks = []
        for r in range(2):
            heads_e = [8 * r + 0, 8 * r + 2, 8 * r + 4, 8 * r + 6]
            heads_o = [h + 1 for h in heads_e]
            if r == 1:
                wt(T, evac_toks[0])
                wt(T, evac_toks[1])
            for si, h in enumerate(heads_e):
                def mk(si=si, h=h):
                    def f(e):
                        m = e.matmul(PSC[:, si * 128:(si + 1) * 128],
                                     head_fm(kfm, h), head_fm(qfm, h), start=True, stop=True)
                        if si == 3:
                            m.then_inc(s_T.h, 1)
                    return f
                T(mk())
            tok_Ae = s_T.inc()
            for si, h in enumerate(heads_o):
                def mk(si=si, h=h):
                    def f(e):
                        m = e.matmul(PSD[:, si * 128:(si + 1) * 128],
                                     head_fm(kfm, h), head_fm(qfm, h), start=True, stop=True)
                        if si == 3:
                            m.then_inc(s_T.h, 1)
                    return f
                T(mk())
            tok_Ao = s_T.inc()
            wt(V, tok_Ae)
            h0 = heads_e[0]
            pre_V(PSC[:, 0:1])
            V(lambda e, h0=h0: e.scalar_tensor_tensor(
                ap3(A_sb, h0 * 128, [[2048, 128], [256, 4], [1, 128]]),
                PSC[:].rearrange("p (s c) -> p s c", c=128), C(6),
                mask4[:].rearrange("p (s c) -> p s c", c=128),
                ALU.mult, ALU.mult))
            evac_toks.append(seal_V(A_sb[:, h0 * 128:h0 * 128 + 1]))
            wt(V, tok_Ao)
            h1 = heads_o[0]
            pre_V(PSD[:, 0:1])
            V(lambda e, h1=h1: e.scalar_tensor_tensor(
                ap3(A_sb, h1 * 128, [[2048, 128], [256, 4], [1, 128]]),
                PSD[:].rearrange("p (s c) -> p s c", c=128), C(6),
                mask4[:].rearrange("p (s c) -> p s c", c=128),
                ALU.mult, ALU.mult))
            evac_toks.append(seal_V(A_sb[:, h1 * 128:h1 * 128 + 1]))
        tok_A = evac_toks[-1]
        wt(T, tok_A)
        wt(T, sbf_tok[scan_idx % 2])
        sold = S_bf[scan_idx % 2]
        for h in range(H):
            bank = PSC if h % 2 == 0 else PSD
            off = (h // 2) * 64
            T(lambda e, h=h, off=off, b=bank: e.matmul(
                b[:, off:off + 64], A_sb[:, h * 128:(h + 1) * 128],
                v_tm[:, h * 64:(h + 1) * 64], start=True, stop=False))
            def mk(h=h, off=off, so=sold, b=bank):
                def f(e):
                    m = e.matmul(b[:, off:off + 64], head_fm(qfm, h),
                                 so[(h % 2) * 64:(h % 2) * 64 + 64,
                                    (h // 2) * 64:(h // 2) * 64 + 64],
                                 start=False, stop=True)
                    if h == H - 1:
                        m.then_inc(s_T.h, 1)
                return f
            T(mk())
        s_T.inc()
        tok_o = pe_guard(PSA, 0)
        wt(V, tok_o)
        V(lambda e: e.tensor_tensor(
            ap3(o_tm, 0, [[1024, 128], [128, 8], [1, 64]]), PSC[:, 0:512],
            ap3(sO, 0, [[16, 128], [2, 8], [0, 64]]), ALU.mult))
        V(lambda e: e.tensor_tensor(
            ap3(o_tm, 64, [[1024, 128], [128, 8], [1, 64]]), PSD[:, 0:512],
            ap3(sO, 1, [[16, 128], [2, 8], [0, 64]]), ALU.mult))
        tok_oev = seal_V(o_tm[:, 0:1])
        wt(SC, tok_oev)
        SC(lambda e: e.activation(sq_scr[:], o_tm[:], AF.Square))
        tok_o2 = seal_SC(sq_scr[:, 0:1])
        wt(V, tok_o2)
        V(lambda e: e.tensor_reduce(sso[:], sq_scr[:].rearrange("p (h e) -> p h e", e=64),
                                    AX.X, ALU.add))
        tok_sso = seal_V(sso[:, 0:1], 2)
        wt(SC, tok_sso)
        emit_rsqrt(ro[:], sso[:], 64.0, 1e-6)
        SC(lambda e: e.activation(sg[:], g_f[:], AF.Silu))
        tok_sg = seal_SC(sg[:, 0:1])
        wt(V, tok_sg)
        V(lambda e: e.tensor_tensor(sq_scr[:], o_tm[:],
                                    ap3(ro, 0, [[16, 128], [1, 16], [0, 64]]), ALU.mult))
        V(lambda e: e.tensor_mul(gated[:], sq_scr[:], sg[:]))
        tok_gated = seal_V(gated[:, 0:1])
        wt(T, ptr_free[0])
        wt(T, tok_gated)
        tok_gtr = mm_tr8(gated)
        wt(V, tok_gtr)
        pre_V(PTR[:, 0:1])
        V(lambda e: e.tensor_copy(gfm[:], PTR[:]))
        ptr_free[0] = tok_gfm = seal_V(gfm[:, 0:1])
        wt(T, tok_gfm)
        for n in range(2):
            bank = PSC if n == 0 else PSD
            for i in range(8):
                def mk(n=n, i=i, b=bank):
                    def f(e):
                        m = e.matmul(b[:, 0:512], gfm[:, i * 128:(i + 1) * 128],
                                     wo_sb[:, i * 1024 + n * 512: i * 1024 + (n + 1) * 512],
                                     start=(i == 0), stop=(i == 7))
                        if n == 1 and i == 7:
                            m.then_inc(s_T.h, 1)
                    return f
                T(mk())
        tok_wo = s_T.inc()
        wt(V, tok_wo)
        wt(V, xf_tok[c])
        pre_V(PSC[:, 0:1])
        pre_V(PSD[:, 0:1])
        V(lambda e, sl=sl: e.scalar_tensor_tensor(
            y1t[:, 0:512], PSC[:, 0:512], 1.0, xf[sl][:, 0:512], ALU.mult, ALU.add))
        V(lambda e, sl=sl: e.scalar_tensor_tensor(
            y1t[:, 512:1024], PSD[:, 0:512], 1.0, xf[sl][:, 512:1024], ALU.mult, ALU.add))
        V(lambda e: e.tensor_reduce(C(9), y1t[:], AX.X, ALU.add))
        tok_resid = seal_V(C(9), 2)
        psc_gate[0] = tok_resid
        psd_gate[0] = tok_resid
        xf_free[c] = tok_resid
        wt(SC, tok_resid)
        SC(lambda e: e.activation(sq_scr[:], y1t[:], AF.Square, accum_out=C(14)))
        SC(lambda e: e.activation(C(10), C(14), AF.Identity, bias=0.0, scale=1.0 / 1024.0))
        SC(lambda e: e.activation(C(11), C(9), AF.Square, bias=0.0, scale=1.0 / 1024.0))
        tok_msq = seal_SC(C(11))
        wt(V, tok_msq)
        V(lambda e: e.tensor_sub(C(11), C(10), C(11)))
        tok_var = seal_V(C(11))
        wt(SC, tok_var)
        SC(lambda e: e.activation(C(12), C(11), AF.Ln, bias=1e-5, scale=1.0))
        SC(lambda e: e.activation(C(12), C(12), AF.Exp, bias=0.0, scale=-0.5))
        SC(lambda e: e.activation(C(13), C(9), AF.Identity, bias=0.0, scale=-1.0 / 1024.0))
        SC(lambda e: e.activation(C(13), C(13), AF.Identity, bias=0.0, scale=C(12)))
        wt(SC, (s_y1, s_y1.n))
        SC(lambda e: e.activation(y1o[:], y1t[:], AF.Identity, bias=C(13), scale=C(12)))
        tok_y1 = seal_SC(y1o[:, 0:1])
        wt(SY, tok_y1)
        SY(lambda e, c=c: e.dma_start(out=y1buf[c * 128:(c + 1) * 128, :], in_=y1o[:])
           .then_inc(s_y1.h, 16))

        s_y1.inc(16)
        if dbg_chunk is not None and c == dbg_chunk:
            bufs = dict(q_tm=q_tm, k_tm=k_tm, v_tm=v_tm, g_f=g_f, lnw=lnw, epos=epos,
                        eneg=eneg, eB=eB, eBmb=eBmb, kdec=kdec, kd_tm=kd_tm, qfm=qfm,
                        kfm=kfm, A_sb=A_sb, o_tm=o_tm, ro=ro, sg=sg, gated=gated,
                        y1t=y1t, cols=cols, sO=sO, sso=sso,
                        S_old=S_bf[scan_idx % 2], S_new=S_sb[(scan_idx + 1) % 2])
            for nm, buf in bufs.items():
                SY(lambda e, d=dbg_d[nm], b=buf: e.dma_start(out=d, in_=b[:])
                   .then_inc(s_dbg.h, 16))
                s_dbg.inc(16)
            wt(T, s_dbg.tok)
            wt(V, s_dbg.tok)
            wt(SC, s_dbg.tok)

    for p in range(2):
        G(lambda e, p=p: e.dma_start(out=xb[p % 2][:], in_=x_pre[p * 128:(p + 1) * 128, :])
          .then_inc(s_xb.h, 16))
        xb_tok[("pre", p)] = s_xb.inc(16)

    scan_idx = 0
    for p in range(N_PRE):
        emit_prefix_chunk(p, scan_idx)
        ser_barrier()
        interleave_quant()
        scan_idx += 1
        nxt = p + 2
        if nxt < N_PRE:
            wt(G, xtr_done[("pre", p)])
            G(lambda e, n=nxt: e.dma_start(out=xb[n % 2][:], in_=x_pre[n * 128:(n + 1) * 128, :])
              .then_inc(s_xb.h, 16))
            xb_tok[("pre", nxt)] = s_xb.inc(16)

    for c in range(2):
        SY(lambda e, c=c: e.dma_start(out=xf[c % 2][:], in_=x_attn[c * 128:(c + 1) * 128, :])
           .then_inc(s_x.h, 16))
        xf_tok[c] = s_x.inc(16)
        wt(G, xtr_done[("pre", N_PRE - 1 - c)])
        G(lambda e, c=c: e.dma_start(out=xb[c % 2][:], in_=x_attn[c * 128:(c + 1) * 128, :])
          .then_inc(s_xb.h, 16))
        xb_tok[("attn", c)] = s_xb.inc(16)

    for c in range(N_ATTN):
        emit_attn_chunk(c, scan_idx)
        ser_barrier()
        interleave_quant()
        scan_idx += 1
        nxt = c + 2
        if nxt < N_ATTN:
            wt(SY, xf_free[c])
            SY(lambda e, n=nxt: e.dma_start(out=xf[n % 2][:], in_=x_attn[n * 128:(n + 1) * 128, :])
               .then_inc(s_x.h, 16))
            xf_tok[nxt] = s_x.inc(16)
            wt(G, xtr_done[("attn", c)])
            G(lambda e, n=nxt: e.dma_start(out=xb[n % 2][:], in_=x_attn[n * 128:(n + 1) * 128, :])
              .then_inc(s_xb.h, 16))
            xb_tok[("attn", nxt)] = s_xb.inc(16)

    y1_total[0] = s_y1.n
    assert q_state["q2"] >= 32, "quant pipeline incomplete"

    wt(T, psm_free[0])
    T(lambda e: e.matmul(PSM[:, 100:101], onesr[:], onesc[0:1, 0:1], start=True, stop=True)
      .then_inc(s_bar.h, 1))
    V(lambda e: e.memset(scal1[:, 0:1], 0.0).then_inc(s_bar.h, 1))
    SC(lambda e: e.memzero(u16[:, 0:1]).then_inc(s_bar.h, 1))
    G(lambda e: e.engine_nop().then_inc(s_bar.h, 1))
    s_bar.inc(4)
    BAR = s_bar.tok
    Y1ALL = s_y1.tok
    QWALL = s_qw.tok
    XBALL = s_xb.tok
    XALL = s_x.tok
    SLALL = s_sl.tok

    a_stack.close()

    b_stack = ExitStack()
    w1q_sb = b_stack.enter_context(nc.sbuf_tensor("w1q_sb", [128, 8 * DH], BF16))
    w2q_sb = b_stack.enter_context(nc.sbuf_tensor("w2q_sb", [128, NCC * 1024], BF16))
    ext = [b_stack.enter_context(nc.sbuf_tensor(f"ext{i}", [128, NCC * 258], BF16))
           for i in range(2)]
    convb = b_stack.enter_context(nc.sbuf_tensor("convb", [128, NCC * 256], BF16))
    y1r = [b_stack.enter_context(nc.sbuf_tensor(f"y1r{i}", [128, 2048], F32)) for i in range(2)]
    ytr1 = b_stack.enter_context(nc.sbuf_tensor("ytr1", [128, 512], F32))
    ytf = b_stack.enter_context(nc.sbuf_tensor("ytf", [128, 512], BF16))
    ytfm = b_stack.enter_context(nc.sbuf_tensor("ytfm", [128, 2048], BF16))
    rscr = b_stack.enter_context(nc.sbuf_tensor("rscr", [128, 512], F32))
    bcols = b_stack.enter_context(nc.sbuf_tensor("bcols", [128, 16], F32))
    y2pre = b_stack.enter_context(nc.sbuf_tensor("y2pre", [128, 1024], F32))
    mxc = y2pre

    def BC(i):
        return bcols[:, i:i + 1]

    for st in (T, V, SC, G):
        wt(st, BAR)
    wt(SY, BAR)
    for st in (T, V, SC):
        wt(st, Y1ALL)
        wt(st, QWALL)
        wt(st, XBALL)
        wt(st, XALL)
        wt(st, SLALL)
    wt(SY, QWALL)
    SY(lambda e: e.dma_start(out=w1q_sb[:].rearrange("p (k w) -> p k w", w=DH),
                             in_=w1q_h.rearrange("(k p) w -> p k w", p=128))
       .then_inc(s_qw.h, 16))
    s_qw.inc(16)
    SY(lambda e: e.dma_start(out=w2q_sb[:].rearrange("p (k w) -> p k w", w=1024),
                             in_=w2q_h.rearrange("(k p) w -> p k w", p=128))
       .then_inc(s_qw.h, 16))
    tok_wqld = s_qw.inc(16)
    wt(T, tok_wqld)

    hq_free = {"A": None, "B": None}
    y1r_free = [None, None]
    y1r_free_sc = [None, None]
    y1r_tok = [None, None]
    pscd_free = [None]
    psm_b_free = [None]
    out_tok = [None]
    ext_sources = {}

    def emit_ytile(w, i):
        ysl = y1r[w % 2]
        wt(V, y1r_tok[w % 2])
        wt(V, s2_sc_tok[0])
        if out_tok[0] is not None:
            wt(V, out_tok[0])
        V(lambda e, i=i: e.tensor_reduce(BC(i), ysl[:, i * 1024:(i + 1) * 1024], AX.X,
                                         ALU.max, apply_absolute_value=True))
        tok_mx1 = seal_V(BC(i), 2)
        wt(SC, tok_mx1)
        SC(lambda e, i=i: e.activation(BC(2 + i), BC(0 + i), AF.Relu, bias=-1e-5, scale=1.0))
        SC(lambda e, i=i: e.activation(BC(2 + i), BC(2 + i), AF.Ln, bias=1e-5, scale=1.0))
        SC(lambda e, i=i: e.activation(BC(4 + i), BC(2 + i), AF.Exp, bias=4.8441870864585871,
                                       scale=-1.0))
        SC(lambda e, i=i: e.activation(BC(0 + i), BC(2 + i), AF.Exp, bias=0.0, scale=1.0))
        SC(lambda e, i=i: e.activation(BC(6 + i), BC(0 + i), AF.Identity, bias=0.0,
                                       scale=a1fac[:]))
        tok_cols = seal_SC(BC(6 + i))
        prev_tr = None
        for hh in range(2):
            if hh == 1:
                wt(SC, tok_ytf)
            def mkr(i=i, hh=hh):
                def f(e):
                    e.activation(ytr1[:], ysl[:, i * 1024 + hh * 512: i * 1024 + (hh + 1) * 512],
                                 AF.Identity, bias=MAGIC, scale=BC(4 + i))
                return f
            SC(mkr())
            tok_r1 = seal_SC(ytr1[:, 0:1])
            wt(V, tok_r1)
            if prev_tr is not None:
                wt(V, prev_tr)
            V(lambda e, i=i: e.tensor_scalar(ytf[:], ytr1[:], -MAGIC, BC(6 + i),
                                             ALU.add, ALU.mult))
            tok_ytf = seal_V(ytf[:, 0:1])
            if hh == 0:
                wt(T, ptr_free[0])
            wt(T, tok_ytf)
            for j in range(4):
                def mkt(j=j, hh=hh):
                    def f(e):
                        m = e.transpose(PTR[:, (hh * 4 + j) * 128:(hh * 4 + j + 1) * 128],
                                        ytf[:, j * 128:(j + 1) * 128], ident[:])
                        if j == 3:
                            m.then_inc(s_T.h, 1)
                    return f
                T(mkt())
            prev_tr = s_T.inc()
        wt(V, prev_tr)
        pre_V(PTR[:, 1023:1024])
        for k in range(8):
            def mkc(k=k, i=i):
                def f(e):
                    e.tensor_copy(ytfm[:, k * 256 + i * 128: k * 256 + (i + 1) * 128],
                                  PTR[:, k * 128:(k + 1) * 128])
                return f
            V(mkc())
        ptr_free[0] = seal_V(ytfm[:, i * 128:i * 128 + 1])
        return ptr_free[0]

    def emit_hwindow(w):
        wt(SY, y1r_free[w % 2])
        wt(SY, y1r_free_sc[w % 2])
        SY(lambda e, w=w: e.dma_start(
            out=y1r[w % 2][:].rearrange("p (i j) -> p i j", j=1024),
            in_=y1buf[w * 256:(w + 1) * 256, :].rearrange("(i p) j -> p i j", p=128))
           .then_inc(s_y1r.h, 16))
        y1r_tok[w % 2] = s_y1r.inc(16)
        tok_fm = None
        for i in range(2):
            tok_fm = emit_ytile(w, i)
        if dbg_win is not None and w == dbg_win:
            wt(SY, tok_fm)
            SY(lambda e: e.dma_start(out=dbg_d["ytfm"], in_=ytfm[:]).then_inc(s_dbg.h, 16))
            s_dbg.inc(16)
            wt(T, s_dbg.tok)
        wt(T, tok_fm)
        for quad in range(8):
            bank = PSA if quad % 2 == 0 else PSB
            bname = "A" if quad % 2 == 0 else "B"
            wt(T, hq_free[bname])
            for j in range(4):
                cc = quad * 4 + j
                for k in range(8):
                    def mk(cc=cc, j=j, k=k, b=bank, last=(j == 3 and k == 7)):
                        def f(e):
                            m = e.matmul(b[:, j * 256:(j + 1) * 256],
                                         w1q_sb[:, k * DH + cc * 128: k * DH + (cc + 1) * 128],
                                         ytfm[:, k * 256:(k + 1) * 256],
                                         start=(k == 0), stop=(k == 7))
                            if last:
                                m.then_inc(s_T.h, 1)
                        return f
                    T(mk())
            tok_quad = s_T.inc()
            wt(V, tok_quad)
            q0 = quad * 4
            pre_V(bank[:, 0:1])
            if w > 0:
                V(lambda e, q0=q0, b=bank, w=w: e.tensor_copy(
                    ext[(w - 1) % 2][:].rearrange("p (g c) -> p g c", c=258)[:, q0:q0 + 4, 129:258],
                    b[:].rearrange("p (g c) -> p g c", c=256)[:, :, 0:129]))
            def mk2(q0=q0, b=bank, w=w):
                def f(e):
                    e.tensor_copy(
                        ext[w % 2][:].rearrange("p (g c) -> p g c", c=258)[:, q0:q0 + 4, 0:129],
                        b[:].rearrange("p (g c) -> p g c", c=256)[:, :, 127:256])
                return f
            V(mk2())
            hq_free[bname] = seal_V(ext[w % 2][:, q0 * 258:q0 * 258 + 1])
            if w >= 1:
                emit_conv_quad(w - 1, quad, hq_free[bname])
        ext_sources[w] = hq_free["B"]

    head_tok = {}
    w2_tok = {}
    conv_done = {}
    s2_sc_tok = [None]

    def emit_conv_quad(o, q, tok_evac):
        exto = ext[o % 2]
        wt(SC, tok_evac)
        if q == 0 and (o - 1) in w2_tok:
            wt(SC, w2_tok[o - 1])
        for j in range(4):
            cc = q * 4 + j
            def mkA(cc=cc, j=j):
                def f(e):
                    e.activation(convb[:, cc * 256:(cc + 1) * 256],
                                 exto[:, cc * 258 + 1: cc * 258 + 257],
                                 AF.Identity, bias=cb_sb[:, cc:cc + 1],
                                 scale=cw_sb[:, 3 * cc + 1:3 * cc + 2])
                return f
            SC(mkA())
        tok_cA = seal_SC(convb[:, q * 4 * 256: q * 4 * 256 + 1])
        wt(V, tok_cA)
        for j in range(4):
            cc = q * 4 + j
            V(lambda e, cc=cc: e.scalar_tensor_tensor(
                convb[:, cc * 256:(cc + 1) * 256], exto[:, cc * 258: cc * 258 + 256],
                cw_sb[:, 3 * cc:3 * cc + 1], convb[:, cc * 256:(cc + 1) * 256],
                ALU.mult, ALU.add))
        for j in range(4):
            cc = q * 4 + j
            def mkC(cc=cc, j=j):
                def f(e):
                    e.scalar_tensor_tensor(
                        convb[:, cc * 256:(cc + 1) * 256], exto[:, cc * 258 + 2: cc * 258 + 258],
                        cw_sb[:, 3 * cc + 2:3 * cc + 3], convb[:, cc * 256:(cc + 1) * 256],
                        ALU.mult, ALU.add)
                return f
            V(mkC())
        conv_done[o] = seal_V(convb[:, q * 4 * 256: q * 4 * 256 + 1])

    def emit_out_head(o):
        exto = ext[o % 2]
        wt(SC, conv_done[o])
        SC(lambda e: e.activation(convb[:], convb[:], AF.Silu))
        tok_silu = seal_SC(convb[:, 0:1])
        if dbg_win is not None and o == dbg_win:
            wt(SY, tok_silu)
            SY(lambda e: e.dma_start(out=dbg_d["silu"], in_=convb[:]).then_inc(s_dbg.h, 16))
            s_dbg.inc(16)
            wt(V, s_dbg.tok)
        wt(V, tok_silu)
        if out_tok[0] is not None:
            wt(V, out_tok[0])
        V(lambda e: e.tensor_reduce(mxc[:, 0:256],
                                    ap3(convb, 0, [[NCC * 256, 128], [1, 256], [256, NCC]]),
                                    AX.X, ALU.max, apply_absolute_value=True))
        head_tok[o] = seal_V(mxc[:, 0:1], 2)

    def emit_out_tail(o):
        exto = ext[o % 2]
        tok_mx = head_tok[o]
        wt(T, tok_mx)
        wt(T, psm_b_free[0])
        T(lambda e: e.matmul(PSM[:, 0:128], mxc[:, 0:128], identf[:], start=True, stop=True))
        T(lambda e: e.matmul(PSM[:, 128:256], mxc[:, 128:256], identf[:], start=True, stop=True)
          .then_inc(s_T.h, 1))
        s_T.inc()
        tok_mxT = pe_guard(PSM, 256)
        wt(V, tok_mxT)
        V(lambda e: e.tensor_reduce(BC(0), PSM[:, 0:128], AX.X, ALU.max))
        V(lambda e: e.tensor_reduce(BC(1), PSM[:, 128:256], AX.X, ALU.max))
        tok_mx2 = seal_V(BC(1), 2)
        wt(SC, tok_mx2)
        for i in range(2):
            SC(lambda e, i=i: e.activation(BC(2 + i), BC(0 + i), AF.Relu, bias=-1e-5, scale=1.0))
            SC(lambda e, i=i: e.activation(BC(2 + i), BC(2 + i), AF.Ln, bias=1e-5, scale=1.0))
            SC(lambda e, i=i: e.activation(BC(4 + i), BC(2 + i), AF.Exp, bias=4.8441870864585871,
                                           scale=-1.0))
            SC(lambda e, i=i: e.activation(BC(0 + i), BC(2 + i), AF.Exp, bias=0.0, scale=1.0))
            def mk(i=i):
                def f(e):
                    e.activation(BC(8 + i), BC(0 + i), AF.Identity, bias=0.0,
                                 scale=a2fac[:])
                return f
            SC(mk())
        tok_s2 = seal_SC(BC(5))
        s2_sc_tok[0] = tok_s2
        wt(T, tok_s2)
        T(lambda e: e.matmul(PSM[0:1, 256:384], BC(4), identf[:], start=True, stop=True))
        T(lambda e: e.matmul(PSM[0:1, 384:512], BC(5), identf[:], start=True, stop=True)
          .then_inc(s_T.h, 1))
        s_T.inc()
        tok_row = pe_guard(PSM, 0)
        wt(V, tok_row)
        V(lambda e: e.tensor_copy(y2pre[0:1, 256:384], PSM[0:1, 256:384]))
        V(lambda e: e.tensor_copy(y2pre[0:1, 384:512], PSM[0:1, 384:512]))
        tok_rowc = seal_V(y2pre[0:1, 256:257], p=1)
        wt(T, tok_rowc)
        T(lambda e: e.matmul(PSM[:, 0:256], onesr[:], y2pre[0:1, 256:512],
                             start=True, stop=True).then_inc(s_T.h, 1))
        s_T.inc()
        tok_s2bm = pe_guard(PSM, 256)
        wt(V, tok_s2bm)
        V(lambda e: e.tensor_copy(y2pre[:, 256:512], PSM[:, 0:256]))
        psm_b_free[0] = seal_V(y2pre[:, 256:257])
        wt(V, psm_b_free[0])
        s2b_ap = ap3(y2pre, 256, [[1024, 128], [0, 2], [1, 256]])
        for grp in range(16):
            q0 = grp * 2
            V(lambda e, q0=q0: e.tensor_tensor(
                rscr[:].rearrange("p (g c) -> p g c", g=2),
                convb[:].rearrange("p (g c) -> p g c", c=256)[:, q0:q0 + 2, :],
                s2b_ap, ALU.mult))
            def mk(q0=q0):
                def f(e):
                    e.tensor_scalar(
                        convb[:].rearrange("p (g c) -> p g c", c=256)[:, q0:q0 + 2, :],
                        rscr[:].rearrange("p (g c) -> p g c", g=2),
                        MAGIC, -MAGIC, ALU.add, ALU.add)
                return f
            V(mk())
        tok_h2q = seal_V(convb[:, 0:1])
        if dbg_win is not None and o == dbg_win:
            wt(SY, tok_h2q)
            SY(lambda e: e.dma_start(out=dbg_d["h2q"], in_=convb[:]).then_inc(s_dbg.h, 16))
            SY(lambda e: e.dma_start(out=dbg_d["bcols"], in_=bcols[:]).then_inc(s_dbg.h, 16))
            SY(lambda e: e.dma_start(out=dbg_d["s2b"], in_=y2pre[:, 256:512])
               .then_inc(s_dbg.h, 16))
            s_dbg.inc(48)
            wt(T, s_dbg.tok)
        for i in range(2):
            wt(T, tok_h2q)
            wt(T, pscd_free[0])
            for cc in range(NCC):
                T(lambda e, cc=cc, i=i: e.matmul(
                    PSC[:, 0:512], convb[:, cc * 256 + i * 128: cc * 256 + (i + 1) * 128],
                    w2q_sb[:, cc * 1024: cc * 1024 + 512],
                    start=(cc == 0), stop=(cc == NCC - 1)))
            for cc in range(NCC):
                def mk(cc=cc, i=i):
                    def f(e):
                        m = e.matmul(
                            PSD[:, 0:512],
                            convb[:, cc * 256 + i * 128: cc * 256 + (i + 1) * 128],
                            w2q_sb[:, cc * 1024 + 512: cc * 1024 + 1024],
                            start=(cc == 0), stop=(cc == NCC - 1))
                        if cc == NCC - 1:
                            m.then_inc(s_T.h, 1)
                    return f
                T(mk())
            tok_w2 = s_T.inc()
            w2_tok[o] = tok_w2
            tau = 2 * o + 1 + i
            slot = (tau // 2) % 2
            off = (tau % 2) * 1024
            wt(V, tok_w2)
            wt(V, y1r_tok[slot])
            if out_tok[0] is not None:
                wt(V, out_tok[0])
            pre_V(PSC[:, 0:1])
            pre_V(PSD[:, 0:1])
            V(lambda e, i=i, slot=slot, off=off: e.scalar_tensor_tensor(
                y2pre[:, 0:512], PSC[:, 0:512], BC(8 + i),
                y1r[slot][:, off:off + 512], ALU.mult, ALU.add))
            V(lambda e, i=i, slot=slot, off=off: e.scalar_tensor_tensor(
                y2pre[:, 512:1024], PSD[:, 0:512], BC(8 + i),
                y1r[slot][:, off + 512:off + 1024], ALU.mult, ALU.add))
            V(lambda e: e.tensor_reduce(BC(12), y2pre[:], AX.X, ALU.add))
            pscd_free[0] = tok_pre = seal_V(BC(12), 2)
            if dbg_win is not None and o == dbg_win and i == 0:
                wt(SY, tok_pre)
                SY(lambda e: e.dma_start(out=dbg_d["y2a"], in_=y2pre[:]).then_inc(s_dbg.h, 16))
                s_dbg.inc(16)
                wt(SC, s_dbg.tok)
            wt(SC, tok_pre)
            SC(lambda e, o=o: e.activation(y1r[o % 2][:, 0:1024], y2pre[:], AF.Square,
                                           accum_out=BC(13)))
            SC(lambda e: e.activation(BC(13), BC(13), AF.Identity, bias=0.0, scale=1.0 / 1024.0))
            SC(lambda e: e.activation(BC(14), BC(12), AF.Square, bias=0.0, scale=1.0 / 1024.0))
            tok_msq = seal_SC(BC(14))
            wt(V, tok_msq)
            V(lambda e: e.tensor_sub(BC(14), BC(13), BC(14)))
            tok_var = seal_V(BC(14))
            wt(SC, tok_var)
            SC(lambda e: e.activation(BC(14), BC(14), AF.Ln, bias=1e-5, scale=1.0))
            SC(lambda e: e.activation(BC(14), BC(14), AF.Exp, bias=0.0, scale=-0.5))
            SC(lambda e: e.activation(BC(15), BC(12), AF.Identity, bias=0.0, scale=-1.0 / 1024.0))
            SC(lambda e: e.activation(BC(15), BC(15), AF.Identity, bias=0.0, scale=BC(14)))
            SC(lambda e: e.activation(y2pre[:], y2pre[:], AF.Identity, bias=BC(15), scale=BC(14)))
            tok_out = seal_SC(y2pre[:, 0:1])
            wt(SY, tok_out)
            row0 = o * 256 + i * 128
            SY(lambda e, row0=row0: e.dma_start(out=out_d[row0:row0 + 128, :], in_=y2pre[:])
               .then_inc(s_out.h, 16))
            out_tok[0] = s_out.inc(16)
        y1r_free[o % 2] = s_V.tok
        y1r_free_sc[o % 2] = s_S.tok

    for w in range(N_WIN):
        if w >= 2:
            emit_out_tail(w - 2)
        emit_hwindow(w)
        if w >= 1:
            emit_out_head(w - 1)
    emit_out_tail(N_WIN - 2)

    SY(lambda e: e.wait_ge(s_out.h, s_out.n))
    b_stack.close()

    with nc.Block() as block:
        @block.tensor
        def _(eng):
            for f in st_T:
                f(eng)

        @block.vector
        def _(eng):
            for f in st_V:
                f(eng)

        @block.scalar
        def _(eng):
            for f in st_S:
                f(eng)

        @block.sync
        def _(eng):
            for f in st_Y:
                f(eng)

        @block.gpsimd
        def _(eng):
            for f in st_G:
                f(eng)

    g_stack.close()
    return nc


def _consts():
    j = np.arange(128)
    ident = np.eye(128, dtype=bf16)
    identf = np.eye(128, dtype=np.float32)
    mask1 = (j[:, None] <= j[None, :]).astype(bf16)
    mask4 = np.tile(mask1, (1, 4))
    tri = np.where(j[:, None] <= j[None, :], np.float32(-1.0 / 16.0), np.float32(0.0))
    onesf = np.full((128, 128), -1.0 / 16.0, dtype=np.float32)
    onesr = np.ones((1, 128), dtype=np.float32)
    onesc = np.ones((128, 1), dtype=np.float32)
    kcol = np.tile(np.asarray(KCOL_VALS, dtype=np.float32)[None, :], (128, 1))
    return dict(ident=ident, identf=identf, mask4=np.ascontiguousarray(mask4),
                tri=np.ascontiguousarray(tri.astype(np.float32)), onesf=onesf,
                onesr=onesr, onesc=onesc, kcol=np.ascontiguousarray(kcol))


def make_inmaps(L0_attn_x, Wq, Wk, Wv, Wg, Wgt, Wo, w1, conv_w, conv_b, w2):
    consts = _consts()
    w1t = np.ascontiguousarray(np.asarray(w1).T.astype(np.float32))
    w2t = np.ascontiguousarray(np.asarray(w2).T.astype(np.float32))
    cw = np.ascontiguousarray(
        np.asarray(conv_w)[:, 0, :].reshape(32, 128, 3).transpose(1, 0, 2).reshape(128, 96)
        .astype(np.float32))
    cb = np.ascontiguousarray(np.asarray(conv_b).reshape(32, 128).T.astype(np.float32))
    x = np.asarray(L0_attn_x)
    z128 = np.zeros((128, D), dtype=np.float32)
    zpre = np.zeros((N_PRE * 128, D), dtype=np.float32)
    in_maps = []
    for core in range(8):
        b, half = divmod(core, 2)
        if half == 0:
            xa = np.concatenate([z128, x[b, 0:4224]], axis=0)
            xp = zpre
        else:
            xa = np.concatenate([x[b, 3968:8192], z128], axis=0)
            xp = np.ascontiguousarray(x[b, 0:3968])
        m = dict(x_attn=np.ascontiguousarray(xa.astype(np.float32)), x_pre=xp,
                 wq=np.ascontiguousarray(np.asarray(Wq, dtype=np.float32)),
                 wk=np.ascontiguousarray(np.asarray(Wk, dtype=np.float32)),
                 wv=np.ascontiguousarray(np.asarray(Wv, dtype=np.float32)),
                 wg=np.ascontiguousarray(np.asarray(Wg, dtype=np.float32)),
                 wgt=np.ascontiguousarray(np.asarray(Wgt, dtype=np.float32)),
                 wo=np.ascontiguousarray(np.asarray(Wo, dtype=np.float32)),
                 w1t=w1t, w2t=w2t, cw=cw, cb=cb, **consts)
        in_maps.append(m)
    return in_maps


def kernel(L0_attn_x, L0_attn_Wq, L0_attn_Wk, L0_attn_Wv, L0_attn_Wg, L0_attn_Wgt,
           L0_attn_Wo, L0_ffn_w1, L0_ffn_conv_w, L0_ffn_conv_b, L0_ffn_w2):
    in_maps = make_inmaps(L0_attn_x, L0_attn_Wq, L0_attn_Wk, L0_attn_Wv, L0_attn_Wg,
                          L0_attn_Wgt, L0_attn_Wo, L0_ffn_w1, L0_ffn_conv_w,
                          L0_ffn_conv_b, L0_ffn_w2)
    nc = build_nc()
    res = run_bass_kernel_spmd(nc, in_maps, core_ids=list(range(8)))
    out = np.empty((B, S, D), dtype=np.float32)
    for core in range(8):
        b, half = divmod(core, 2)
        out[b, half * 4096:(half + 1) * 4096] = np.asarray(res.results[core]["out"],
                                                           dtype=np.float32)
    return out
